# version 09
# baseline (speedup 1.0000x reference)
import numpy as np
from contextlib import ExitStack

import concourse.bass as bass
import concourse.mybir as mybir
from concourse.bass_utils import run_bass_kernel_spmd

F32 = mybir.dt.float32
BF16 = mybir.dt.bfloat16
AF = mybir.ActivationFunctionType
ALU = mybir.AluOpType
AX = mybir.AxisListType

D = 1024
T = 2048
KT = 8
NH = 16
DK = 64
WIN = 256
NWIN = T // WIN
FCH = 512
NCH = T // FCH
DI = 2730
FDEPTH = 6
LN_EPS = 1e-5
NEG = -1e30

ENGS = ("sync", "gpsimd", "tensor", "vector", "scalar")

DEBUG = {"dump": [], "blocks": None, "fdepth": FDEPTH}


class WStream:

    def __init__(self, mk, cast_eng="scalar"):
        self.mk = mk
        self.i = 0
        self.prev_cast = None
        self.cons = [None, None]

    def load(self, jobs, cols, extra_wait=()):
        mk = self.mk
        buf = self.i % 2
        wl = list(extra_wait)
        if self.prev_cast is not None:
            wl.append(("scalar", self.prev_cast))
        first = True
        for dst, srcap in jobs:
            mk.dma("sync", dst, srcap, waits=wl if first else ())
            first = False
        ld = mk.last("sync")
        wlc = [("sync", ld)]
        if self.cons[buf] is not None:
            wlc.append(("tensor", self.cons[buf]))
        cd = mk.op("scalar", lambda e, buf=buf, cols=cols: e.copy(
            out=mk.wbf[buf][:, 0:cols], in_=mk.wst[:, 0:cols]),
            waits=wlc)
        self.prev_cast = cd
        self.i += 1
        self._last_buf = buf
        return mk.wbf[buf], cd

    def consumed(self, t_ms):
        self.cons[self._last_buf] = t_ms


class MK:
    def __init__(self, debug_dump=(), blocks=None, fdepth=FDEPTH):
        self.nc = bass.Bass("TRN2", target_bir_lowering=False, debug=False,
                            dynamic_dma_scratch_size=512)
        self.es = ExitStack()
        self.q = {e: [] for e in ENGS}
        self.c = {e: 0 for e in ENGS}
        self.sems = {e: self.es.enter_context(self.nc.semaphore("sem_" + e))
                     for e in ENGS}

        self.consts = {}
        self.dram = {}
        self.debug_dump = set(debug_dump)
        self.blocks = set(blocks) if blocks is not None else {
            "attn_a", "ffn1", "attn_b", "ffn2"}
        self.fdepth = fdepth
        self.dumped = []
        self._alloc()

    VSG = ("vector", "scalar", "gpsimd")

    def op(self, eng, fn, waits=(), inc=0, is_dma=False):
        selfwait = -1
        if eng in self.VSG:
            selfwait = self.c[eng]
            if not is_dma:
                inc = 1
        if is_dma:
            selfwait = self.c[eng]
        if inc:
            self.c[eng] += inc
        self.q[eng].append((fn, tuple(waits), inc, self.c[eng], selfwait))
        return self.c[eng]

    def last(self, eng):
        return self.c[eng]

    def dma(self, eng, out_ap, in_ap, waits=(), **kw):
        return self.op(eng, lambda e: e.dma_start(out=out_ap, in_=in_ap, **kw),
                       waits=waits, inc=16, is_dma=True)

    def build(self):
        nc = self.nc
        sems = self.sems
        with nc.Block() as blk:
            for e in ENGS:
                ops = self.q[e]

                def make_body(ops=ops, myeng=e):
                    def body(engobj):
                        waited = {}
                        owaited = -1
                        for fn, waits, inc, val, selfwait in ops:
                            for (we, wv) in waits:
                                if we == myeng:
                                    continue
                                if waited.get(we, -1) < wv:
                                    engobj.wait_ge(sems[we], wv)
                                    waited[we] = wv
                            if selfwait > 0 and selfwait > owaited:
                                engobj.wait_ge(sems[myeng], selfwait)
                                owaited = selfwait
                            ins = fn(engobj)
                            if inc:
                                ins.then_inc(sems[myeng], inc)
                    return body
                getattr(blk, e)(make_body())
        return nc

    def sbt(self, name, shape, dtype):
        return self.es.enter_context(self.nc.sbuf_tensor(name, shape, dtype))

    def pst(self, name, shape, dtype=F32):
        return self.es.enter_context(self.nc.psum_tensor(name, shape, dtype))

    def hbm(self, name, shape, dtype=F32, kind="ExternalInput"):
        ap = self.nc.dram_tensor(name, list(shape), dtype, kind=kind).ap()
        self.dram[name] = ap
        return ap

    def const(self, name, arr):
        arr = np.ascontiguousarray(arr.astype(np.float32))
        self.consts[name] = arr
        return self.hbm(name, arr.shape, F32)

    def dump(self, name, sbuf_ap, waits):
        if name not in self.debug_dump:
            return
        out = self.hbm("dbg_" + name, list(sbuf_ap.shape), sbuf_ap.dtype,
                       kind="ExternalOutput")
        self.dumped.append("dbg_" + name)
        d = self.dma("scalar", out, sbuf_ap, waits=waits)
        self.op("vector", lambda e: e.engine_nop(), waits=[("scalar", d)])
        self.op("tensor", lambda e: e.nop(), waits=[("scalar", d)])

    def _alloc(self):
        self.resid = self.sbt("resid", [128, KT * T], F32)
        self.xb = self.sbt("xb", [128, KT * T], BF16)
        self.sq = self.sbt("sq", [128, KT * T], BF16)
        self.arena = self.sbt("arena", [128, 16 * 1024], BF16)
        self.ar2 = self.sbt("ar2", [128, 12 * 1024], BF16)
        self.wst = self.sbt("wst", [128, 2048], F32)
        self.wbf = [self.sbt("wbf0", [128, 2048], BF16),
                    self.sbt("wbf1", [128, 2048], BF16)]
        self.strow = self.sbt("strow", [128, T], F32)
        self.srow16 = self.sbt("srow16", [128, 2 * T], BF16)
        self.smallf = self.sbt("smallf", [128, 128], F32)
        self.ident = self.sbt("ident", [128, 128], BF16)
        self.csmall = self.sbt("csmall", [128, 960], BF16)
        self.perm = self.sbt("perm", [128, 128], BF16)
        self.trim = self.sbt("trim", [128, 128], BF16)
        self.masks = self.sbt("masks", [128, 1664], BF16)
        self.pbig = self.pst("pbig", [128, 8 * FCH], F32)
        self.pb = [self.pbig[:, i * FCH:(i + 1) * FCH] for i in range(8)]

    def rk(self, k):
        return self.resid[:, k * T:(k + 1) * T]

    def bk(self, buf, k):
        return buf[:, k * T:(k + 1) * T]


CS_ONECOL = 0
CS_SEL2 = 8
CS_C2T = 12
CS_KBSEL = 140
CS_ONEROW = 268
CS_KBSELP = 780


def cs(mk, off, n):
    return mk.csmall[:, off:off + n]


def m1(mk, it):
    return mk.masks[:, 128 - 128 * it:640 - 128 * it]


def m1w0(mk, it):
    return mk.masks[:, 640 + 128 - 128 * it:640 + 384 - 128 * it]


def m2(mk, jt):
    return mk.masks[:, 1024 + 384 - 128 * jt:1024 + 640 - 128 * jt]



def wsrc3(sl, p=128):
    return sl.rearrange("(kt p) f -> p kt f", p=p)


def wdst3(stage_slice, kt):
    return stage_slice.rearrange("p (kt f) -> p kt f", kt=kt)


def emit_cast_shift(mk, src_tiles, out_buf, shifted, waits=()):
    for k in range(KT):
        src = src_tiles[k]
        dst = mk.bk(out_buf, k)
        wl = waits if k == 0 else ()
        if shifted and k >= 4:
            mk.op("vector", lambda e, d=dst: e.memset(d[:, 0:1], 0.0), waits=wl)
            mk.op("vector", lambda e, s=src, d=dst:
                  e.tensor_copy(out=d[:, 1:T], in_=s[:, 0:T - 1]))
        else:
            mk.op("vector", lambda e, s=src, d=dst:
                  e.tensor_copy(out=d, in_=s), waits=wl)
    return mk.op("vector", lambda e: e.engine_nop(), inc=1)


def emit_ln(mk, raw_buf, out_buf, g_row=None, b_row=None, cast_done=None,
            extra_wait=(), out_tiles_f32=None):
    assert out_buf is None or out_buf is not raw_buf

    def sqk(k):
        if k < 6:
            return mk.ar2[:, k * T:(k + 1) * T]
        return mk.arena[:, (k - 6) * T:(k - 5) * T]

    wl0 = list(extra_wait)
    if cast_done is not None:
        wl0.append(("vector", cast_done))
    for k in range(KT):
        mk.op("scalar", lambda e, k=k:
              e.activation(out=sqk(k), in_=mk.bk(raw_buf, k), func=AF.Square),
              waits=wl0 if k == 0 else (), inc=(1 if k == KT - 1 else 0))
    sq_done = mk.last("scalar")

    onec = cs(mk, CS_ONECOL, 1)
    arF0 = mk.arena[:].bitcast(F32)
    ev = []
    for c in range(NCH):
        bank_s = mk.pb[6]
        bank_q = mk.pb[7]
        wl = [("scalar", sq_done)] + list(extra_wait)
        if cast_done is not None:
            wl.append(("vector", cast_done))
        if c >= 1:
            wl.append(("scalar", ev[c - 1]))
        for k in range(KT):
            mk.op("tensor", lambda e, k=k, c=c, bank_s=bank_s:
                  e.matmul(out=bank_s[0:1, :], lhsT=onec,
                           rhs=mk.bk(raw_buf, k)[:, c * FCH:(c + 1) * FCH],
                           start=(k == 0), stop=(k == KT - 1)),
                  waits=wl if k == 0 else ())
            mk.op("tensor", lambda e, k=k, c=c, bank_q=bank_q:
                  e.matmul(out=bank_q[0:1, :], lhsT=onec,
                           rhs=sqk(k)[:, c * FCH:(c + 1) * FCH],
                           start=(k == 0), stop=(k == KT - 1)),
                  inc=(1 if k == KT - 1 else 0))
        t_mm = mk.last("tensor")
        mk.op("scalar", lambda e, c=c, bank_s=bank_s:
              e.activation(out=mk.strow[0:1, c * FCH:(c + 1) * FCH],
                           in_=bank_s[0:1, :], func=AF.Copy, scale=1.0 / D),
              waits=[("tensor", t_mm)])
        e_ev = mk.op("scalar", lambda e, c=c, bank_q=bank_q:
                     e.activation(out=arF0[0:1, 2304 + c * FCH:2304 + (c + 1) * FCH],
                                  in_=bank_q[0:1, :], func=AF.Copy, scale=1.0 / D),
                     inc=1)
        ev.append(e_ev)
    s_stats = mk.last("scalar")
    arF = mk.arena[:].bitcast(F32)
    MU = mk.strow[0:1, 0:T]
    MSQ = arF[0:1, 2304:2304 + T]
    VAR = arF[0:1, 4352:4352 + T]
    mk.op("vector", lambda e: e.tensor_mul(out=VAR, in0=MU, in1=MU),
          waits=[("scalar", s_stats)])
    v_var = mk.op("vector", lambda e: e.tensor_sub(out=VAR, in0=MSQ, in1=VAR),
                  inc=1)
    LNV = MSQ
    mk.op("scalar", lambda e: e.activation(out=LNV, in_=VAR, func=AF.Ln,
                                           bias=mk.smallf[0:1, 110:111]),
          waits=[("vector", v_var)])
    RSTD = mk.srow16[0:1, 0:T]
    s_rstd = mk.op("scalar", lambda e: e.activation(out=RSTD, in_=LNV,
                                                    func=AF.Exp, scale=-0.5),
                   inc=1)
    NMR = mk.srow16[0:1, T:2 * T]
    v_rows = mk.op("vector", lambda e: e.scalar_tensor_tensor(
        out=NMR, in0=MU, scalar=-1.0, in1=RSTD, op0=ALU.mult, op1=ALU.mult),
        waits=[("scalar", s_rstd)], inc=1)

    ones_r = cs(mk, CS_ONEROW, FCH)[0:1, :]
    t1 = mk.arena[:, 2 * T:2 * T + FCH]
    prev = [None, None]
    for c in range(NCH):
        for k in range(KT):
            idx = (c * KT + k) % 2
            bank_s = mk.pb[4 if idx == 0 else 2]
            bank_t = mk.pb[5 if idx == 0 else 3]
            lg = g_row[:, k * 128:(k + 1) * 128] if g_row is not None \
                else ones_r[:, 0:128]
            wl = [("vector", v_rows)]
            if prev[idx] is not None:
                wl.append(("vector", prev[idx]))
            mk.op("tensor", lambda e, lg=lg, c=c, bank_s=bank_s:
                  e.matmul(out=bank_s[:, :], lhsT=lg,
                           rhs=RSTD[:, c * FCH:(c + 1) * FCH],
                           start=True, stop=True),
                  waits=wl)
            mk.op("tensor", lambda e, lg=lg, c=c, bank_t=bank_t:
                  e.matmul(out=bank_t[:, :], lhsT=lg,
                           rhs=NMR[:, c * FCH:(c + 1) * FCH],
                           start=True, stop=(b_row is None)),
                  inc=(0 if b_row is not None else 1))
            if b_row is not None:
                br = b_row[:, k * 128:(k + 1) * 128]
                mk.op("tensor", lambda e, br=br, bank_t=bank_t:
                      e.matmul(out=bank_t[:, :], lhsT=br, rhs=ones_r,
                               start=False, stop=True), inc=1)
            t_bc = mk.last("tensor")
            rawsl = mk.bk(raw_buf, k)[:, c * FCH:(c + 1) * FCH]
            if out_buf is not None:
                outsl = mk.bk(out_buf, k)[:, c * FCH:(c + 1) * FCH]
            else:
                outsl = out_tiles_f32[k][:, c * FCH:(c + 1) * FCH]
            mk.op("vector", lambda e, rawsl=rawsl, bank_s=bank_s:
                  e.tensor_tensor(out=t1, in0=rawsl, in1=bank_s[:, :], op=ALU.mult),
                  waits=[("tensor", t_bc)])
            prev[idx] = mk.op("vector", lambda e, outsl=outsl, bank_t=bank_t:
                              e.tensor_tensor(out=outsl, in0=t1,
                                              in1=bank_t[:, :], op=ALU.add), inc=1)
    return mk.last("vector")


def emit_gemm_b(mk, w_col_ap, n_tiles, xsrc, out_cb, n0=0, cast_eng="vector",
                x_wait=(), tag=""):
    evac = {}
    for nt in range(n_tiles):
        wbuf, cd = mk.ws.load([(wdst3(mk.wst[:, 0:KT * 128], KT),
                                w_col_ap(nt))], KT * 128,
                              extra_wait=list(x_wait))
        for c in range(NCH):
            slot = (nt * NCH + c) % 4
            bank = mk.pb[slot]
            wl = [("scalar", cd)] + list(x_wait)
            if (nt * NCH + c) >= 4:
                wl.append(evac[(nt * NCH + c) - 4])
            for k in range(KT):
                last = (k == KT - 1)
                mk.op("tensor", lambda e, k=k, c=c, bank=bank, last=last,
                      wbuf=wbuf:
                      e.matmul(out=bank[:, :],
                               lhsT=wbuf[:, k * 128:(k + 1) * 128],
                               rhs=xsrc(k)[:, c * FCH:(c + 1) * FCH],
                               start=(k == 0), stop=last),
                      waits=wl if k == 0 else (), inc=(1 if last else 0))
            t_mm = mk.last("tensor")
            evac[nt * NCH + c] = out_cb(n0 + nt, c, bank, t_mm)
        mk.ws.consumed(mk.last("tensor"))
    return


def emit_gemm_a(mk, w_row_ap, n_chunks, nch_cols, xsrc, out_cb, x_wait=(),
                cast_eng="scalar", kt=KT, tag=""):
    evac = {}
    TT = T // 128
    for c in range(n_chunks):
        wbuf, cd = mk.ws.load([(wdst3(mk.wst[:, 0:kt * nch_cols], kt),
                                w_row_ap(c))], kt * nch_cols,
                              extra_wait=list(x_wait))
        for tt in range(TT):
            slot = (c * TT + tt) % 4
            bank = mk.pb[slot]
            wl = [("scalar", cd)] + list(x_wait)
            if (c * TT + tt) >= 4:
                wl.append(evac[(c * TT + tt) - 4])
            for k in range(kt):
                last = (k == kt - 1)
                mk.op("tensor", lambda e, k=k, tt=tt, bank=bank, last=last,
                      wbuf=wbuf:
                      e.matmul(out=bank[:, 0:nch_cols],
                               lhsT=xsrc(k)[:, tt * 128:(tt + 1) * 128],
                               rhs=wbuf[:, k * nch_cols:(k + 1) * nch_cols],
                               start=(k == 0), stop=last),
                      waits=wl if k == 0 else (), inc=(1 if last else 0))
            t_mm = mk.last("tensor")
            evac[c * TT + tt] = out_cb(c, tt, bank, t_mm)
        mk.ws.consumed(mk.last("tensor"))
    return


def wcol_b(mk, name, n_off, ncols=128):
    ap = mk.dram[name]

    def f(nt):
        off = n_off + nt * ncols
        sl = ap[:, off:off + ncols]
        return sl.rearrange("(kt p) f -> p (kt f)", p=128)
    return f


def emit_resid_add(mk, nt, c, bank, t_mm, extra=None, scale_ap=None):
    dst = mk.rk(nt)[:, c * FCH:(c + 1) * FCH]
    if extra is None:
        m = mk.op("vector", lambda e, dst=dst, bank=bank:
                  e.tensor_tensor(out=dst, in0=dst, in1=bank[:, :], op=ALU.add),
                  waits=[("tensor", t_mm)], inc=1)
    else:
        raise NotImplementedError
    return ("vector", m)



def emit_load_input(mk):
    xin = mk.hbm("x_t", [D, T], F32)
    for k in range(KT):
        mk.dma("scalar", mk.rk(k), xin[k * 128:(k + 1) * 128, :])
    return mk.last("scalar")


def emit_store_output(mk, waits):
    out = mk.hbm("out_t", [D, T], F32, kind="ExternalOutput")
    for k in range(KT):
        mk.dma("scalar", out[k * 128:(k + 1) * 128, :], mk.rk(k), waits=waits)
    g_done = mk.last("scalar")
    mk.op("sync", lambda e: e.nop(), waits=[("scalar", g_done)])


def emit_load_consts(mk):
    fs = mk.hbm("c_fsmall", [128, 8], F32)
    vm = mk.op("vector", lambda e: e.memset(mk.smallf[:, :], 0.0))
    mk.dma("sync", mk.smallf[:, 110:118], fs, waits=[("vector", vm)])
    jobs = [("c_ident16", [128, 128], mk.ident[:, :]),
            ("c_small16", [128, 960], mk.csmall[:, :]),
            ("c_masks16", [128, 1664], mk.masks[:, :]),
            ("c_perm16", [128, 128], mk.perm[:, :]),
            ("c_trim16", [128, 128], mk.trim[:, :])]
    for name, shape, dst in jobs:
        srcap = mk.hbm(name, shape, BF16)
        mk.dma("sync", dst, srcap)
    mk.hbm("c_tab16", [128, 4 * T], BF16)
    return mk.last("sync")


def build_program(debug_dump=(), blocks=None, fdepth=FDEPTH):
    mk = MK(debug_dump=debug_dump, blocks=blocks, fdepth=fdepth)
    mk.ws = WStream(mk)
    ld = emit_load_input(mk)
    cst = emit_load_consts(mk)
    mk.hbm("park", [D, T], F32, kind="Internal")
    mk.hbm("ot_scratch", [D, T], BF16, kind="Internal")
    if "attn_a" in mk.blocks:
        mk.hbm("aa_wqkvT", [D, 3 * D])
        mk.hbm("aa_woT", [D, D])
        mk.hbm("aa_wltm", [NH, DK, DK])
        mk.hbm("aa_wltmT", [NH, DK, DK])
        for nm in ("V_gs", "V_T0", "beta_tau", "beta_gm", "C_ch", "gamma"):
            mk.hbm("aa_" + nm, [NH])
        mk.hbm("aa_alpha_ppd", [1])
        mk.hbm("aa_IC_threshold", [1])
    if "attn_b" in mk.blocks:
        mk.hbm("ab_qT", [D, DK])
        mk.hbm("ab_kT", [D, DK])
        mk.hbm("ab_vT", [D, D])
        mk.hbm("ab_oT", [D, D])
        mk.hbm("ab_f1T", [D, 4 * D])
        mk.hbm("ab_f2T", [4 * D, D])
        for nm, sh in (("ln1_w", [D]), ("ln1_b", [D]), ("ln2_w", [D]),
                       ("ln2_b", [D]), ("q_b", [DK]), ("k_b", [DK]),
                       ("v_b", [D]), ("o_b", [D]), ("f1_b", [4 * D]),
                       ("f2_b", [D])):
            mk.hbm("ab_" + nm, sh)
    if "ffn1" in mk.blocks or "ffn2" in mk.blocks:
        mk.hbm("ff_Wqkv", [FDEPTH, D, 3 * D])
        mk.hbm("ff_q_scale", [FDEPTH, DK])
        mk.hbm("ff_k_scale", [FDEPTH, DK])
        mk.hbm("ff_Wo", [FDEPTH, D, D])
        mk.hbm("ff_ln_g", [FDEPTH, D])
        mk.hbm("ff_ln_b", [FDEPTH, D])
        mk.hbm("ff_W1", [FDEPTH, D, 2 * DI])
        mk.hbm("ff_b1", [FDEPTH, 2 * DI])
        mk.hbm("ff_W2", [FDEPTH, DI, D])
        mk.hbm("ff_b2", [FDEPTH, D])

    waits = [("scalar", ld), ("sync", cst)]
    if "lntest" in mk.blocks:
        cd = emit_cast_shift(mk, [mk.rk(k) for k in range(KT)], mk.sq,
                             shifted=("lnshift" in mk.blocks), waits=waits)
        vd = emit_ln(mk, mk.sq, mk.xb, cast_done=cd, extra_wait=tuple(waits))
        for k in range(KT):
            mk.dump(f"ln{k}", mk.bk(mk.xb, k), [("vector", vd)])
    from_blocks = []
    if "attn_a" in mk.blocks:
        waits = emit_attn_a(mk, waits)
    if "ffn1" in mk.blocks:
        waits = emit_ffn_module(mk, waits, call=0)
    if "attn_b" in mk.blocks:
        waits = emit_attn_b(mk, waits)
    if "ffn2" in mk.blocks:
        waits = emit_ffn_module(mk, waits, call=1)
    emit_store_output(mk, waits)
    return mk


def emit_attn_a(mk, waits):
    SEG = 1024
    onec = cs(mk, CS_ONECOL, 1)
    arF = mk.arena[:].bitcast(F32)
    ar2F = mk.ar2[:].bitcast(F32)
    sqF = mk.sq[:].bitcast(F32)

    VN = arF[0:16, 0:4096]
    SP = arF[0:16, 4096:8192]
    GCH = ar2F[0:16, 0:4096]
    T4 = sqF[0:16, 0:4096]
    COLS = {}
    names = ("V_gs", "V_T0", "beta_tau", "beta_gm", "C_ch", "gamma")
    for i, nm in enumerate(names):
        COLS[nm] = mk.smallf[0:16, 32 + i:33 + i]
        mk.dma("sync", COLS[nm], mk.dram["aa_" + nm].rearrange("(p o) -> p o", o=1),
               waits=list(waits) if i == 0 else (),
               allow_slow_non_contiguous=True)
    COLS["alpha"] = mk.smallf[0:16, 40:41]
    COLS["ic"] = mk.smallf[0:16, 41:42]
    mk.dma("sync", COLS["alpha"],
           mk.dram["aa_alpha_ppd"].rearrange("(o p) -> o p", o=1)
           .broadcast_to([16, 1]), allow_slow_non_contiguous=True)
    mk.dma("sync", COLS["ic"],
           mk.dram["aa_IC_threshold"].rearrange("(o p) -> o p", o=1)
           .broadcast_to([16, 1]), allow_slow_non_contiguous=True)
    mk.dma("sync", VN, mk.dram["aa_wltm"].rearrange("h d e -> h (d e)"))
    lcols = mk.last("sync")
    v0 = mk.op("vector", lambda e: e.tensor_scalar(
        out=VN, in0=VN, scalar1=COLS["V_gs"], scalar2=COLS["V_T0"],
        op0=ALU.add, op1=ALU.subtract), waits=[("sync", lcols)])
    mk.op("scalar", lambda e: e.activation(out=SP, in_=VN, func=AF.Abs),
          waits=[("vector", v0)])
    mk.op("scalar", lambda e: e.activation(out=SP, in_=SP, func=AF.Exp,
                                           scale=-1.0))
    mk.op("scalar", lambda e: e.activation(out=SP, in_=SP, func=AF.Ln,
                                           bias=1.0))
    s1 = mk.op("scalar", lambda e: e.activation(out=T4, in_=VN, func=AF.Relu))
    v1 = mk.op("vector", lambda e: e.tensor_add(out=SP, in0=SP, in1=T4),
               waits=[("scalar", s1)])
    mk.op("vector", lambda e: e.reciprocal(out=mk.smallf[0:16, 42:43],
                                           in_=COLS["C_ch"]))
    mk.op("vector", lambda e: e.tensor_scalar(
        out=GCH, in0=SP, scalar1=COLS["beta_tau"], scalar2=None,
        op0=ALU.mult, op1=ALU.bypass))
    v3 = mk.op("vector", lambda e: e.tensor_scalar(
        out=T4, in0=GCH, scalar1=mk.smallf[0:16, 42:43], scalar2=-1.0,
        op0=ALU.mult, op1=ALU.mult))
    s2 = mk.op("scalar", lambda e: e.activation(out=T4, in_=T4, func=AF.Exp),
               waits=[("vector", v3)])
    rc = mk.hbm("aa_rc", [2, 16, 4096], F32, kind="Internal")
    d1 = mk.dma("scalar", rc[0], T4, waits=[("scalar", s2)])
    v4 = mk.op("vector", lambda e: e.tensor_scalar(
        out=T4, in0=GCH, scalar1=COLS["ic"], scalar2=COLS["alpha"],
        op0=ALU.subtract, op1=ALU.mult), waits=[("scalar", d1)])
    s3 = mk.op("scalar", lambda e: e.activation(out=T4, in_=T4, func=AF.Tanh),
               waits=[("vector", v4)])
    s4 = mk.op("scalar", lambda e: e.activation(out=VN, in_=VN,
                                                func=AF.Sigmoid))
    v5 = mk.op("vector", lambda e: e.tensor_mul(out=SP, in0=SP, in1=VN),
               waits=[("scalar", s4)])
    mk.op("vector", lambda e: e.tensor_scalar(
        out=SP, in0=SP, scalar1=COLS["beta_gm"], scalar2=None,
        op0=ALU.mult, op1=ALU.bypass))
    v7 = mk.op("vector", lambda e: e.tensor_mul(out=SP, in0=SP, in1=T4),
               waits=[("scalar", s3)])
    v8 = mk.op("vector", lambda e: e.tensor_scalar(
        out=SP, in0=SP, scalar1=COLS["gamma"], scalar2=None,
        op0=ALU.mult, op1=ALU.bypass))
    d2 = mk.dma("scalar", rc[1], SP, waits=[("vector", v8)])
    pre_done = [("scalar", d2)]

    cd = emit_cast_shift(mk, [mk.rk(k) for k in range(KT)], mk.sq,
                         shifted=False, waits=list(waits) + pre_done)
    vd = emit_ln(mk, mk.sq, mk.xb, cast_done=cd, extra_wait=pre_done)
    x_wait = [("vector", vd)]

    VBSEL = mk.arena[:, 8192:8192 + 4096]
    MSUM = mk.arena[:, 13312:13312 + 2048]
    KBSELR = mk.arena[:, 15616:15616 + 128]
    WL16 = mk.arena[:, 15552:15552 + 64]
    mk.dma("sync", VBSEL, mk.hbm("c_vbsel16", [128, 4096], BF16),
           waits=list(x_wait))
    mk.dma("sync", MSUM, mk.hbm("c_msum16", [128, 2048], BF16))
    ld_sel = mk.dma("sync", KBSELR, mk.hbm("c_kbselr16", [128, 128], BF16))
    sel_wait = [("sync", ld_sel)]

    RCOL = arF[:, 7680:7712]
    CCOL = arF[:, 7712:7744]
    USTATE = arF[:, 7744:7776]

    QRAW = mk.ar2[:, 0:T]
    KN = mk.ar2[:, T:2 * T]
    VRAW = mk.ar2[:, 2 * T:3 * T]
    ZB = mk.ar2[:, 3 * T:3 * T + SEG]
    UB = mk.ar2[:, 3 * T + SEG:3 * T + 2 * SEG]
    Y2 = mk.ar2[:, 3 * T + 2 * SEG:3 * T + 2 * SEG + FCH]
    KBS = mk.ar2[:, 3 * T + 2 * SEG + FCH:3 * T + 2 * SEG + FCH + SEG]
    SC2 = mk.ar2[:, 11776:12288]
    wq = mk.dram["aa_wqkvT"]
    sel2 = cs(mk, CS_SEL2, 2)
    c2t = mk.csmall[0:2, CS_C2T:CS_C2T + 128]
    for hp in range(8):
        for idx, nt_off in enumerate((hp * 128, 1024 + hp * 128,
                                      2048 + hp * 128)):
            dstbuf = (QRAW, KN, VRAW)[idx]
            wap = wsrc3(wq[:, nt_off:nt_off + 128])
            wl = [("tensor", mk.last("tensor")), ("scalar", mk.last("scalar"))]
            ldw = mk.dma("sync", wdst3(mk.wst[:, 0:1024], KT), wap, waits=wl)
            cw = mk.op("scalar", lambda e: e.copy(
                out=mk.wbf[0][:, 0:1024], in_=mk.wst[:, 0:1024]),
                waits=[("sync", ldw)])
            for c in range(NCH):
                bank = mk.pb[4 + c % 2]
                for k in range(KT):
                    mk.op("tensor", lambda e, k=k, c=c, bank=bank:
                          e.matmul(out=bank[:, :],
                                   lhsT=mk.wbf[0][:, k * 128:(k + 1) * 128],
                                   rhs=mk.bk(mk.xb, k)[:, c * FCH:(c + 1) * FCH],
                                   start=(k == 0), stop=(k == KT - 1)),
                          waits=([("scalar", mk.last("scalar")),
                                  ("vector", mk.last("vector"))] + x_wait)
                          if k == 0 else (), inc=(1 if k == KT - 1 else 0))
                t_mm = mk.last("tensor")
                mk.op("scalar", lambda e, c=c, bank=bank, dstbuf=dstbuf:
                      e.activation(out=dstbuf[:, c * FCH:(c + 1) * FCH],
                                   in_=bank[:, :], func=AF.Copy),
                      waits=[("tensor", t_mm)])
        evac_qk = mk.last("scalar")

        MUK = mk.strow[0:2, 0:T]
        MSQK = ar2F[0:2, 4096:6144]
        MU2 = mk.wst[0:2, 0:T]
        ksq = mk.ar2[:, 3 * T:3 * T + 2 * SEG]
        v_sq = mk.op("vector", lambda e: e.tensor_mul(
            out=ksq, in0=KN, in1=KN), waits=[("scalar", evac_qk)])
        for c in range(NCH):
            wl = [("vector", v_sq)]
            if c > 0:
                wl.append(("scalar", mk.last("scalar")))
            mk.op("tensor", lambda e, c=c: e.matmul(
                out=mk.pb[6][0:2, :], lhsT=sel2,
                rhs=KN[:, c * FCH:(c + 1) * FCH], start=True, stop=True),
                waits=wl)
            t2 = mk.op("tensor", lambda e, c=c, ksq=ksq: e.matmul(
                out=mk.pb[7][0:2, :], lhsT=sel2,
                rhs=ksq[:, c * FCH:(c + 1) * FCH], start=True, stop=True),
                inc=1)
            mk.op("scalar", lambda e, c=c: e.activation(
                out=MUK[:, c * FCH:(c + 1) * FCH], in_=mk.pb[6][0:2, :],
                func=AF.Copy, scale=1.0 / DK), waits=[("tensor", t2)])
            mk.op("scalar", lambda e, c=c: e.activation(
                out=MSQK[:, c * FCH:(c + 1) * FCH], in_=mk.pb[7][0:2, :],
                func=AF.Copy, scale=1.0 / DK), inc=1)
        s_st = mk.last("scalar")
        v10a = mk.op("vector", lambda e: e.tensor_mul(out=MU2, in0=MUK, in1=MUK),
                     waits=[("scalar", s_st)])
        v10 = mk.op("vector", lambda e: e.tensor_sub(out=MSQK, in0=MSQK,
                                                     in1=MU2))
        mk.op("scalar", lambda e: e.activation(
            out=MSQK, in_=MSQK, func=AF.Ln, bias=mk.smallf[0:2, 110:111]),
            waits=[("vector", v10)])
        RSK = mk.srow16[0:2, 0:T]
        s6 = mk.op("scalar", lambda e: e.activation(
            out=RSK, in_=MSQK, func=AF.Exp, scale=-0.5))
        NMK = mk.srow16[0:2, T:2 * T]
        v11 = mk.op("vector", lambda e: e.scalar_tensor_tensor(
            out=NMK, in0=MUK, scalar=-1.0, in1=RSK, op0=ALU.mult,
            op1=ALU.mult), waits=[("scalar", s6)])
        for c in range(NCH):
            mk.op("tensor", lambda e, c=c: e.matmul(
                out=mk.pb[6][:, :], lhsT=c2t, rhs=RSK[0:2, c * FCH:(c + 1) * FCH],
                start=True, stop=True), waits=[("vector", v11),
                                               ("vector", mk.last("vector"))])
            t_bb = mk.op("tensor", lambda e, c=c: e.matmul(
                out=mk.pb[7][:, :], lhsT=c2t, rhs=NMK[0:2, c * FCH:(c + 1) * FCH],
                start=True, stop=True), inc=1)
            mk.op("vector", lambda e, c=c: e.tensor_tensor(
                out=SC2, in0=KN[:, c * FCH:(c + 1) * FCH], in1=mk.pb[6][:, :],
                op=ALU.mult), waits=[("tensor", t_bb)])
            mk.op("vector", lambda e, c=c: e.tensor_tensor(
                out=KN[:, c * FCH:(c + 1) * FCH], in0=SC2, in1=mk.pb[7][:, :],
                op=ALU.add))
        kn_done = mk.last("vector")

        for h2 in range(2):
            h = 2 * hp + h2
            hr = h2 * 64
            ldr = mk.dma("sync", RCOL,
                         mk.dram["aa_rc"][0, h].rearrange("(t p) -> p t", p=128),
                         waits=pre_done + [("vector", mk.last("vector"))],
                         allow_slow_non_contiguous=True)
            mk.dma("sync", CCOL,
                   mk.dram["aa_rc"][1, h].rearrange("(t p) -> p t", p=128),
                   allow_slow_non_contiguous=True)
            mk.dma("sync", mk.wst[0:64, 0:64], mk.dram["aa_wltmT"][h])
            ld_state = mk.last("sync")
            cwl = mk.op("vector", lambda e: e.tensor_copy(
                out=WL16[0:64, :], in_=mk.wst[0:64, 0:64]),
                waits=[("sync", ld_state)])
            if h2 == 1:
                ld_state = mk.dma("sync", WL16[64:128, :], WL16[0:64, :],
                                  waits=[("vector", cwl)])
                wlview = WL16[64:128, :]
            else:
                wlview = WL16[0:64, :]
            for seg in range(2):
                t0 = seg * SEG
                t_kb = mk.op("tensor", lambda e, hr=hr, t0=t0: e.matmul(
                    out=mk.pb[0][:, :],
                    lhsT=KBSELR[hr:hr + 64, :], rhs=KN[hr:hr + 64, t0:t0 + 512],
                    start=True, stop=True),
                    waits=[("vector", kn_done), ("sync", ld_state),
                           ("vector", cwl), ("vector", mk.last("vector")),
                           ("scalar", mk.last("scalar"))] + sel_wait)
                mk.op("tensor", lambda e, hr=hr, t0=t0: e.matmul(
                    out=mk.pb[1][:, :],
                    lhsT=KBSELR[hr:hr + 64, :],
                    rhs=KN[hr:hr + 64, t0 + 512:t0 + 1024],
                    start=True, stop=True))
                mk.op("tensor", lambda e, hr=hr, t0=t0: e.matmul(
                    out=mk.pb[2][:, :],
                    lhsT=KBSELR[hr:hr + 64, :], rhs=QRAW[hr:hr + 64, t0:t0 + 512],
                    start=True, stop=True))
                t_qb = mk.op("tensor", lambda e, hr=hr, t0=t0: e.matmul(
                    out=mk.pb[3][:, :],
                    lhsT=KBSELR[hr:hr + 64, :],
                    rhs=QRAW[hr:hr + 64, t0 + 512:t0 + 1024],
                    start=True, stop=True), inc=1)
                mk.op("scalar", lambda e: e.activation(
                    out=KBS[:, 0:512], in_=mk.pb[0][:, :], func=AF.Copy),
                    waits=[("tensor", t_qb)])
                s_kb = mk.op("scalar", lambda e: e.activation(
                    out=KBS[:, 512:1024], in_=mk.pb[1][:, :], func=AF.Copy))
                for c2 in range(2):
                    mk.op("tensor", lambda e, c2=c2, hr=hr, t0=t0,
                          wlview=wlview: e.matmul(
                        out=mk.pb[4 + c2][hr:hr + 64, :],
                        lhsT=wlview,
                        rhs=QRAW[hr:hr + 64, t0 + c2 * FCH:t0 + (c2 + 1) * FCH],
                        start=True, stop=False,
                        tile_position=(hr, hr)),
                        waits=[("scalar", s_kb)],
                        inc=(1 if c2 == 1 else 0))
                for t in range(32):
                    for c2 in range(2):
                        mk.op("tensor", lambda e, t=t, c2=c2, hr=hr, t0=t0:
                              e.matmul(out=mk.pb[6 + c2][:, :],
                                       lhsT=VBSEL[hr:hr + 64,
                                                  t * 128:(t + 1) * 128],
                                       rhs=VRAW[hr:hr + 64,
                                                t0 + c2 * FCH:t0 + (c2 + 1) * FCH],
                                       start=True, stop=True),
                              waits=[("vector", mk.last("vector"))]
                              if c2 == 0 else (),
                              inc=(1 if c2 == 1 else 0))
                    t_vb = mk.last("tensor")
                    for c2 in range(2):
                        mk.op("vector", lambda e, c2=c2:
                              e.tensor_tensor(
                                  out=ZB[:, c2 * 512:(c2 + 1) * 512],
                                  in0=mk.pb[6 + c2][:, :],
                                  in1=KBS[:, c2 * 512:(c2 + 1) * 512],
                                  op=ALU.mult),
                              waits=[("tensor", t_vb)] if c2 == 0 else ())
                    ini = 0.0 if seg == 0 else USTATE[:, t:t + 1]
                    mk.op("vector", lambda e, t=t, ini=ini:
                          e.tensor_tensor_scan(
                              out=UB, data0=RCOL[:, t:t + 1].broadcast_to([128, SEG]),
                              data1=ZB, initial=ini, op0=ALU.mult, op1=ALU.add))
                    if seg == 0:
                        mk.op("vector", lambda e, t=t: e.tensor_copy(
                            out=USTATE[:, t:t + 1], in_=UB[:, SEG - 1:SEG]))
                    for c2 in range(2):
                        v_y2 = mk.op("vector", lambda e, c2=c2, t=t:
                                     e.scalar_tensor_tensor(
                                         out=Y2,
                                         in0=UB[:, c2 * 512:(c2 + 1) * 512],
                                         scalar=CCOL[:, t:t + 1],
                                         in1=mk.pb[2 + c2][:, :],
                                         op0=ALU.mult, op1=ALU.mult),
                                     waits=[("tensor", mk.last("tensor"))])
                        mk.op("tensor", lambda e, t=t, c2=c2, hr=hr:
                              e.matmul(out=mk.pb[4 + c2][hr:hr + 64, :],
                                       lhsT=MSUM[:, t * 64:(t + 1) * 64],
                                       rhs=Y2, start=False,
                                       stop=(t == 31),
                                       tile_position=(0, hr)),
                              waits=[("vector", v_y2)], inc=1)
                t_y = mk.last("tensor")
                for c2 in range(2):
                    mk.op("scalar", lambda e, c2=c2, hr=hr, hp=hp, t0=t0:
                          e.activation(
                              out=mk.sq[hr:hr + 64,
                                        hp * 2048 + t0 + c2 * FCH:
                                        hp * 2048 + t0 + (c2 + 1) * FCH],
                              in_=mk.pb[4 + c2][hr:hr + 64, :], func=AF.Copy),
                          waits=[("tensor", t_y)], inc=1)
    aa_done = mk.last("scalar")

    def wo_ap(nt):
        return wsrc3(mk.dram["aa_woT"][:, nt * 128:(nt + 1) * 128])

    emit_gemm_b(mk, wo_ap, 8, lambda k: mk.sq[:, k * 2048:(k + 1) * 2048],
                lambda nt, c, bank, t_mm: emit_resid_add(mk, nt, c, bank, t_mm),
                x_wait=[("scalar", aa_done), ("vector", mk.last("vector"))],
                cast_eng="scalar")
    return [("vector", mk.last("vector"))]


def emit_park(mk, waits):
    park = mk.dram["park"]
    for k in range(KT):
        mk.dma("scalar", park[k * 128:(k + 1) * 128, :], mk.rk(k), waits=waits)
    return mk.last("scalar")


def emit_unpark_add(mk, waits):
    park = mk.dram["park"]
    prev_add = None
    for k in range(KT):
        wl = list(waits)
        if prev_add is not None:
            wl.append(("vector", prev_add))
        ld = mk.dma("sync", mk.wst[:, :], park[k * 128:(k + 1) * 128, :],
                    waits=wl + [("scalar", mk.last("scalar"))])
        prev_add = mk.op("vector", lambda e, k=k:
                         e.tensor_tensor(out=mk.rk(k), in0=mk.rk(k),
                                         in1=mk.wst[:, :], op=ALU.add),
                         waits=[("sync", ld)])
    return mk.last("vector")


def ffn_attn_layer(mk, li, waits, call):
    wq = mk.dram["ff_Wqkv"]
    wo = mk.dram["ff_Wo"]
    ots = mk.dram["ot_scratch"]
    CSQ = mk.sq[:, 8192:8192 + T]
    SSQ = mk.sq[:, 8192 + T:8192 + 2 * T]
    CSK = mk.sq[:, 8192 + 2 * T:8192 + 3 * T]
    SSK = mk.sq[:, 8192 + 3 * T:8192 + 4 * T]
    QRAW = mk.ar2[:, 0:T]
    KRAW = mk.ar2[:, T:2 * T]
    SCR = mk.ar2[:, 2 * T:4 * T]
    QT = mk.ar2[:, 4 * T:5 * T]
    KTT = mk.ar2[:, 5 * T:6 * T]
    onec = cs(mk, CS_ONECOL, 1)
    ones_r = cs(mk, CS_ONEROW, FCH)[0:1, :]
    c2t = mk.csmall[0:2, CS_C2T:CS_C2T + 128]
    kbsel = cs(mk, CS_KBSEL, 128)[0:64, :]
    sel2 = cs(mk, CS_SEL2, 2)

    ld_tab = mk.dma("sync", mk.sq[:, 8192:16384], mk.dram["c_tab16"],
                    waits=waits)
    qsc = mk.smallf[:, 100:101]
    ksc = mk.smallf[:, 101:102]
    qscp = mk.smallf[:, 104:105]
    kscp = mk.smallf[:, 105:106]
    kbselp = cs(mk, CS_KBSELP, 128)[0:64, :]
    for i2, (name, dstcol, dstcolp) in enumerate(
            (("ff_q_scale", qsc, qscp), ("ff_k_scale", ksc, kscp))):
        stg = mk.smallf[:, 102 + i2:103 + i2]
        stg16 = mk.srow16[:, 4094 + i2:4095 + i2]
        srcap = mk.dram[name][li, :].rearrange("(p o) -> p o", o=1)
        ld = mk.dma("sync", stg[0:64, :], srcap, waits=list(waits),
                    allow_slow_non_contiguous=True)
        vc = mk.op("vector", lambda e, stg=stg, stg16=stg16:
                   e.tensor_copy(out=stg16[0:64, :], in_=stg[0:64, :]),
                   waits=[("sync", ld)])
        qksc = 8.0 if i2 == 0 else 1.0
        for sel, dc in ((kbsel, dstcol), (kbselp, dstcolp)):
            tm = mk.op("tensor", lambda e, stg16=stg16, sel=sel: e.matmul(
                out=mk.pb[0][:, 0:1], lhsT=sel, rhs=stg16[0:64, :],
                start=True, stop=True),
                waits=[("vector", vc), ("scalar", mk.last("scalar"))], inc=1)
            mk.op("scalar", lambda e, dc=dc, qksc=qksc: e.activation(
                out=dc, in_=mk.pb[0][:, 0:1], func=AF.Copy, scale=qksc),
                waits=[("tensor", tm)])
    setup_done = [("scalar", mk.last("scalar")), ("sync", ld_tab)]

    cd = emit_cast_shift(mk, [mk.rk(k) for k in range(KT)], mk.xb, shifted=True,
                         waits=list(waits))
    x_wait = [("vector", cd)]

    def v_w_ap(c):
        sl = wq[li, :, 2048 + c * 256:2048 + (c + 1) * 256]
        return wsrc3(sl)

    def v_out(c, tt, bank, t_mm):
        dst = mk.arena[:, tt * 1024 + c * 256:tt * 1024 + (c + 1) * 256]
        m = mk.op("scalar", lambda e, dst=dst, bank=bank:
                  e.activation(out=dst, in_=bank[:, 0:256], func=AF.Copy),
                  waits=[("tensor", t_mm)])
        return ("scalar", m)

    emit_gemm_a(mk, v_w_ap, 4, 256, lambda k: mk.bk(mk.xb, k), v_out,
                x_wait=x_wait + setup_done, cast_eng="scalar")
    seed = ("scalar", mk.last("scalar"))
    mk.aux_ev = {3: seed, 7: seed}

    for hp in range(8):
        for idx, nt_off in enumerate((hp * 128, 1024 + hp * 128)):
            dstbuf = QRAW if idx == 0 else KRAW
            wap = wsrc3(wq[li, :, nt_off:nt_off + 128])
            wbuf, cw = mk.ws.load([(wdst3(mk.wst[:, 0:1024], KT), wap)], 1024,
                                  extra_wait=list(waits))
            for c in range(NCH):
                bid = 3 if c % 2 == 0 else 7
                bank = mk.pb[bid]
                for k in range(KT):
                    wl2 = []
                    if k == 0:
                        wl2 = [("scalar", cw), mk.aux_ev[bid]] + x_wait
                    mk.op("tensor", lambda e, k=k, c=c, bank=bank, wbuf=wbuf:
                          e.matmul(out=bank[:, :],
                                   lhsT=wbuf[:, k * 128:(k + 1) * 128],
                                   rhs=mk.bk(mk.xb, k)[:, c * FCH:(c + 1) * FCH],
                                   start=(k == 0), stop=(k == KT - 1)),
                          waits=wl2, inc=(1 if k == KT - 1 else 0))
                t_mm = mk.last("tensor")
                sm = mk.op("scalar", lambda e, c=c, bank=bank, dstbuf=dstbuf:
                           e.activation(out=dstbuf[:, c * FCH:(c + 1) * FCH],
                                        in_=bank[:, :], func=AF.Copy),
                           waits=[("tensor", t_mm)])
                mk.aux_ev[bid] = ("scalar", sm)
            mk.ws.consumed(mk.last("tensor"))
        evac_qk = mk.last("scalar")

        for idx, (raw, dstT, CT, ST, scol, scolp) in enumerate(
                ((QRAW, QT, CSQ, SSQ, qsc, qscp),
                 (KRAW, KTT, CSK, SSK, ksc, kscp))):
            sqq = SCR[:, 0:T]
            v_sq = mk.op("vector", lambda e, raw=raw, sqq=sqq:
                         e.tensor_mul(out=sqq, in0=raw, in1=raw),
                         waits=[("scalar", evac_qk)])
            for c in range(NCH):
                bid = 3 if c % 2 == 0 else 7
                wl = [("vector", v_sq), mk.aux_ev[bid]]
                tm = mk.op("tensor", lambda e, c=c, sqq=sqq, bid=bid:
                           e.matmul(out=mk.pb[bid][0:2, :], lhsT=sel2,
                                    rhs=sqq[:, c * FCH:(c + 1) * FCH],
                                    start=True, stop=True),
                           waits=wl, inc=1)
                sm = mk.op("scalar", lambda e, c=c, bid=bid: e.activation(
                    out=mk.strow[0:2, c * FCH:(c + 1) * FCH],
                    in_=mk.pb[bid][0:2, :], func=AF.Copy),
                    waits=[("tensor", tm)])
                mk.aux_ev[bid] = ("scalar", sm)
            s_ss = mk.last("scalar")
            SS = mk.strow[0:2, 0:T]
            v_mx = mk.op("vector", lambda e: e.tensor_scalar_max(
                out=SS, in0=SS, scalar1=1e-24), waits=[("scalar", s_ss)])
            mk.op("scalar", lambda e: e.activation(out=SS, in_=SS, func=AF.Ln),
                  waits=[("vector", v_mx)])
            RN = mk.srow16[0:2, 0:T]
            s_rn = mk.op("scalar", lambda e: e.activation(
                out=RN, in_=SS, func=AF.Exp, scale=-0.5))
            RNB = SCR[:, T:2 * T]
            for c in range(NCH):
                bid = 3 if c % 2 == 0 else 7
                t_rnb = mk.op("tensor", lambda e, c=c, bid=bid: e.matmul(
                    out=mk.pb[bid][:, :], lhsT=c2t,
                    rhs=RN[0:2, c * FCH:(c + 1) * FCH], start=True, stop=True),
                    waits=[("scalar", s_rn), mk.aux_ev[bid]], inc=1)
                sm = mk.op("scalar", lambda e, c=c, RNB=RNB, bid=bid,
                           t_rnb=t_rnb: e.activation(
                    out=RNB[:, c * FCH:(c + 1) * FCH], in_=mk.pb[bid][:, :],
                    func=AF.Copy), waits=[("tensor", t_rnb)])
                mk.aux_ev[bid] = ("scalar", sm)
            s_rnb = mk.last("scalar")
            A = SCR[:, 0:T]
            v_A = mk.op("vector", lambda e, A=A, CT=CT, scol=scol, RNB=RNB:
                        e.scalar_tensor_tensor(out=A, in0=CT, scalar=scol,
                                               in1=RNB, op0=ALU.mult,
                                               op1=ALU.mult),
                        waits=[("scalar", s_rnb)])
            v_t1 = mk.op("vector", lambda e, raw=raw, dstT=dstT, A=A:
                         e.tensor_mul(out=dstT, in0=raw, in1=A))
            B = SCR[:, 0:T]
            v_B = mk.op("vector", lambda e, B=B, ST=ST, scolp=scolp, RNB=RNB:
                        e.scalar_tensor_tensor(out=B, in0=ST, scalar=scolp,
                                               in1=RNB, op0=ALU.mult,
                                               op1=ALU.mult))
            t2 = SCR[:, T:2 * T]
            for c in range(NCH):
                bid = 3 if c % 2 == 0 else 7
                t_pm = mk.op("tensor", lambda e, c=c, raw=raw, bid=bid:
                             e.matmul(
                    out=mk.pb[bid][:, :], lhsT=mk.perm[:, :],
                    rhs=raw[:, c * FCH:(c + 1) * FCH], start=True, stop=True),
                    waits=[("vector", v_B), mk.aux_ev[bid]], inc=1)
                vm = mk.op("vector", lambda e, c=c, t2=t2, B=B, bid=bid:
                           e.tensor_tensor(
                    out=t2[:, c * FCH:(c + 1) * FCH], in0=mk.pb[bid][:, :],
                    in1=B[:, c * FCH:(c + 1) * FCH], op=ALU.mult),
                    waits=[("tensor", t_pm)])
                mk.aux_ev[bid] = ("vector", vm)
            mk.op("vector", lambda e, dstT=dstT, t2=t2:
                  e.tensor_tensor(out=dstT, in0=dstT, in1=t2, op=ALU.add))
        prep_done = mk.last("vector")
        if hp == 0:
            mk.dump("qt_hp0", QT, [("vector", prep_done)])
            mk.dump("kt_hp0", KTT, [("vector", prep_done)])

        if not hasattr(mk, "attn_state"):
            mk.attn_state = {"exps": {}, "evac": {}, "wi": 0}
        ast = mk.attn_state
        wins = []
        for h2 in range(2):
            for w in range(NWIN):
                p = ast["wi"] % 2
                ast["wi"] += 1
                h = 2 * hp + h2
                hr = h2 * 64
                wins.append(dict(
                    h=h, hr=hr, w=w, p=p,
                    qh=QT[hr:hr + 64, :], kh=KTT[hr:hr + 64, :],
                    P1=mk.pb[0 if p == 0 else 4],
                    ST0=mk.pb[1 if p == 0 else 5],
                    ST1=mk.pb[2 if p == 0 else 6],
                    ST01=mk.pbig[:, FCH:3 * FCH] if p == 0
                    else mk.pbig[:, 5 * FCH:7 * FCH],
                    PT=SCR[:, p * 1024:(p + 1) * 1024],
                    PSC=SCR[:, 2048 + p * 512:2048 + (p + 1) * 512],
                    NMC=SCR[:, 3776 + p * 32:3776 + p * 32 + 4],
                    DCOL=SCR[:, 3776 + p * 32 + 4:3776 + p * 32 + 8],
                    LNC=SCR[:, 3776 + p * 32 + 8:3776 + p * 32 + 12],
                    NMR=SCR[:, 3200 + p * 256:3200 + (p + 1) * 256],
                    j0=max(0, (w - 1) * WIN),
                    jlen=WIN if w == 0 else 2 * WIN,
                    i0=w * WIN, njt=2 if w == 0 else 4))

        def front_it(c, it):
            PB = c["P1"] if it == 0 else c["ST1"]
            NMC, PSC = c["NMC"], c["PSC"]
            dcf = c["DCOL"].bitcast(F32)
            jlen, j0, i0, w, p = c["jlen"], c["j0"], c["i0"], c["w"], c["p"]
            wl = [("vector", prep_done)]
            key = ("p1", p) if it == 0 else ("st", p)
            if key in ast["exps"]:
                wl.append(("scalar", ast["exps"][key]))
            mk.op("tensor", lambda e, it=it, c=c, jlen=jlen, j0=j0, i0=i0,
                  PB=PB: e.matmul(out=PB[:, 0:jlen],
                                  lhsT=c["qh"][:, i0 + it * 128:
                                               i0 + it * 128 + 128],
                                  rhs=c["kh"][:, j0:j0 + jlen],
                                  start=True, stop=False),
                  waits=wl)
            msk = m1(mk, it) if w > 0 else m1w0(mk, it)
            t_s1 = mk.op("tensor", lambda e, msk=msk, jlen=jlen, PB=PB:
                         e.matmul(out=PB[:, 0:jlen], lhsT=mk.ident[:, :],
                                  rhs=msk[:, 0:jlen],
                                  start=False, stop=True), inc=1)
            v_m = mk.op("vector", lambda e, it=it, jlen=jlen, PB=PB, NMC=NMC:
                        e.reduce_max(out=NMC[:, it:it + 1], in_=PB[:, 0:jlen],
                                     axis=AX.X, negate=True),
                        waits=[("tensor", t_s1)])
            mk.op("scalar", lambda e, it=it, jlen=jlen, PB=PB, NMC=NMC,
                  PSC=PSC, dcf=dcf:
                  e.activation(out=PSC[:, 0:jlen], in_=PB[:, 0:jlen],
                               func=AF.Exp, bias=NMC[:, it:it + 1],
                               accum_out=dcf[:, it:it + 1]),
                  waits=[("vector", v_m)])

        def front_b0(c):
            LNC, NMC = c["LNC"], c["NMC"]
            dcf = c["DCOL"].bitcast(F32)
            s_l = mk.op("scalar", lambda e, dcf=dcf: e.activation(
                out=dcf[:, 0:2], in_=dcf[:, 0:2], func=AF.Ln))
            c["v_b"] = mk.op("vector", lambda e, LNC=LNC, dcf=dcf, NMC=NMC:
                             e.scalar_tensor_tensor(
                                 out=LNC[:, 0:2], in0=dcf[:, 0:2], scalar=-1.0,
                                 in1=NMC[:, 0:2], op0=ALU.mult, op1=ALU.add),
                             waits=[("scalar", s_l)])

        def front_b1(c):
            P1, LNC, NMR = c["P1"], c["LNC"], c["NMR"]
            for it in range(2):
                mk.op("tensor", lambda e, it=it, P1=P1, LNC=LNC: e.matmul(
                    out=P1[0:1, it * 128:(it + 1) * 128],
                    lhsT=LNC[:, it:it + 1], rhs=mk.ident[:, :],
                    start=(it == 0), stop=(it == 1)),
                    waits=[("vector", c["v_b"])], inc=(1 if it == 1 else 0))
            t_nr = mk.last("tensor")
            s_nr = mk.op("scalar", lambda e, P1=P1, NMR=NMR: e.activation(
                out=NMR[0:1, 0:256], in_=P1[0:1, 0:256], func=AF.Copy),
                waits=[("tensor", t_nr)])
            ast["exps"][("p1", c["p"])] = s_nr
            c["s_nr"] = s_nr

        def back_st(c):
            ST0, ST1, NMR, PT = c["ST0"], c["ST1"], c["NMR"], c["PT"]
            njt, j0, i0, w, p = c["njt"], c["j0"], c["i0"], c["w"], c["p"]
            s_nr = c["s_nr"]
            for jt in range(njt):
                bank = ST0 if jt < 2 else ST1
                boff = (jt % 2) * 256
                mjt = jt + 2 if w == 0 else jt
                wl = [("scalar", s_nr)]
                if jt % 2 == 0 and ("st", p) in ast["exps"]:
                    wl.append(("scalar", ast["exps"][("st", p)]))
                if jt == 0 and ("st0", p) in ast["evac"]:
                    wl.append(("scalar", ast["evac"][("st0", p)]))
                mk.op("tensor", lambda e, bank=bank, boff=boff, NMR=NMR:
                      e.matmul(out=bank[:, boff:boff + 256],
                               lhsT=ones_r[:, 0:128], rhs=NMR[0:1, 0:256],
                               start=(boff == 0), stop=False),
                      waits=wl)
                mk.op("tensor", lambda e, bank=bank, boff=boff, jt=jt, c=c,
                      j0=j0, i0=i0:
                      e.matmul(out=bank[:, boff:boff + 256],
                               lhsT=c["kh"][:, j0 + jt * 128:
                                            j0 + jt * 128 + 128],
                               rhs=c["qh"][:, i0:i0 + 256],
                               start=False, stop=False))
                last_of_pair = (jt % 2 == 1) or (w == 0 and jt == njt - 1)
                mk.op("tensor", lambda e, bank=bank, boff=boff, mjt=mjt,
                      lp=last_of_pair:
                      e.matmul(out=bank[:, boff:boff + 256],
                               lhsT=mk.ident[:, :], rhs=m2(mk, mjt),
                               start=False, stop=lp),
                      inc=(1 if last_of_pair else 0))
                if last_of_pair:
                    c["t_st%d" % (jt // 2)] = mk.last("tensor")
            ew = 512 * ((njt + 1) // 2)
            wl = [("tensor", c["t_st0"])]
            if njt > 2:
                wl.append(("tensor", c["t_st1"]))
            s_exp = mk.op("scalar", lambda e, c=c, ew=ew, PT=PT:
                          e.activation(out=PT[:, 0:ew],
                                       in_=c["ST01"][:, 0:ew],
                                       func=AF.Exp),
                          waits=wl)
            ast["exps"][("st", p)] = s_exp
            c["s_exp"] = s_exp

        def back_av(c):
            ST0, PT = c["ST0"], c["PT"]
            njt, j0, hr, h, w, p = (c["njt"], c["j0"], c["hr"], c["h"],
                                    c["w"], c["p"])
            for jt in range(njt):
                jtile = (j0 // 128) + jt
                mk.op("tensor", lambda e, jtile=jtile, jt=jt, h=h, hr=hr,
                      njt=njt, ST0=ST0, PT=PT:
                      e.matmul(out=ST0[hr:hr + 64, 0:256],
                               lhsT=mk.arena[:, jtile * 1024 + h * 64:
                                             jtile * 1024 + h * 64 + 64],
                               rhs=PT[:, jt * 256:(jt + 1) * 256],
                               start=(jt == 0), stop=(jt == njt - 1),
                               tile_position=(0, hr)),
                      waits=[("scalar", c["s_exp"])] if jt == 0 else (),
                      inc=(1 if jt == njt - 1 else 0))
            t_av = mk.last("tensor")
            ev = mk.op("scalar", lambda e, hr=hr, w=w, hp=hp, ST0=ST0:
                       e.activation(out=mk.sq[hr:hr + 64,
                                              (hp % 2) * 2048 + w * WIN:
                                              (hp % 2) * 2048 + (w + 1) * WIN],
                                    in_=ST0[hr:hr + 64, 0:256],
                                    func=AF.Copy),
                       waits=[("tensor", t_av)])
            ast["evac"][("st0", p)] = ev

        for i, c in enumerate(wins):
            if i == 0:
                front_it(c, 0)
                front_it(c, 1)
                front_b0(c)
            nxt = wins[i + 1] if i + 1 < len(wins) else None
            if nxt is not None:
                front_it(nxt, 0)
                front_it(nxt, 1)
            if i > 0:
                back_av(wins[i - 1])
            front_b1(c)
            back_st(c)
            if nxt is not None:
                front_b0(nxt)
        back_av(wins[-1])
        s_ot = mk.last("scalar")
        mk.dma("scalar", ots[hp * 128:(hp + 1) * 128, :],
               mk.sq[:, (hp % 2) * 2048:(hp % 2) * 2048 + 2048],
               waits=[("scalar", s_ot)])
    attn_done = mk.last("scalar")

    for k in range(KT):
        mk.dma("sync", mk.sq[:, k * 2048:(k + 1) * 2048],
               ots[k * 128:(k + 1) * 128, :],
               waits=[("scalar", attn_done)] if k == 0 else ())
    ot_ld = mk.last("sync")

    def wo_ap(nt):
        sl = wo[li, :, nt * 128:(nt + 1) * 128]
        return wsrc3(sl)

    emit_gemm_b(mk, wo_ap, 8, lambda k: mk.sq[:, k * 2048:(k + 1) * 2048],
                lambda nt, c, bank, t_mm: emit_resid_add(mk, nt, c, bank, t_mm),
                x_wait=[("sync", ot_ld)], cast_eng="scalar")
    return [("vector", mk.last("vector"))]


def ffn_ffn_layer(mk, li, waits, call):
    w1 = mk.dram["ff_W1"]
    w2 = mk.dram["ff_W2"]
    G16 = mk.arena[0:1, 12800:12800 + D]
    B16 = mk.arena[0:1, 12800 + D:12800 + 2 * D]
    ldg = mk.dma("sync", mk.strow[0:1, 0:D], mk.dram["ff_ln_g"][li:li + 1, :],
                 waits=list(waits) + [("vector", mk.last("vector"))])
    vg = mk.op("vector", lambda e: e.tensor_copy(out=G16,
                                                 in_=mk.strow[0:1, 0:D]),
               waits=[("sync", ldg)])
    ldb = mk.dma("sync", mk.strow[0:1, D:2 * D],
                 mk.dram["ff_ln_b"][li:li + 1, :], waits=[("vector", vg)])
    aff_done = mk.op("vector", lambda e: e.tensor_copy(
        out=B16, in_=mk.strow[0:1, D:2 * D]), waits=[("sync", ldb)])
    b1 = mk.dram["ff_b1"][li, :]
    b2 = mk.dram["ff_b2"][li, :]
    mk.dma("sync", mk.smallf[:, 0:21],
           b1[0:2688].rearrange("(f p) -> p f", p=128),
           waits=[("vector", aff_done)], allow_slow_non_contiguous=True)
    mk.dma("sync", mk.smallf[0:42, 21:22],
           b1[2688:2730].rearrange("(p o) -> p o", o=1))
    mk.dma("sync", mk.smallf[:, 22:43],
           b1[2730:5418].rearrange("(f p) -> p f", p=128),
           allow_slow_non_contiguous=True)
    mk.dma("sync", mk.smallf[0:42, 43:44],
           b1[5418:5460].rearrange("(p o) -> p o", o=1))
    mk.dma("sync", mk.smallf[:, 44:52], b2.rearrange("(f p) -> p f", p=128),
           allow_slow_non_contiguous=True)
    bias_done = mk.last("sync")

    cd = emit_cast_shift(mk, [mk.rk(k) for k in range(KT)], mk.sq, shifted=True,
                         waits=list(waits))
    vd = emit_ln(mk, mk.sq, mk.xb, g_row=G16, b_row=B16, cast_done=cd,
                 extra_wait=[("vector", aff_done)])
    x_wait = [("vector", vd), ("sync", bias_done)]

    NA = (DI + 127) // 128
    w2ev = {}
    for tc in range(2):
        tok0 = tc * 1024
        for pi in range(NA):
            aoff = pi * 128
            awid = min(128, DI - aoff)
            goff = DI + aoff
            jobs = []
            for wi, noff in enumerate((aoff, goff)):
                wap = wsrc3(w1[li, :, noff:noff + awid])
                jobs.append((wdst3(mk.wst[:, wi * 1024:wi * 1024 + KT * awid],
                                   KT), wap))
            wbuf, cw = mk.ws.load(jobs, 2048, extra_wait=list(waits))
            for wi in range(2):
                for c2 in range(2):
                    bank = mk.pb[wi * 2 + c2]
                    for k in range(KT):
                        wl2 = []
                        if k == 0:
                            wl2 = [("scalar", cw)] + x_wait + \
                                [("vector", mk.last("vector"))]
                        mk.op("tensor", lambda e, wi=wi, c2=c2, k=k, bank=bank,
                              awid=awid, tok0=tok0, wbuf=wbuf:
                              e.matmul(out=bank[0:awid, 0:FCH],
                                       lhsT=wbuf[:, wi * 1024 + k * awid:
                                                 wi * 1024 + k * awid + awid],
                                       rhs=mk.bk(mk.xb, k)[:, tok0 + c2 * FCH:
                                                           tok0 + (c2 + 1) * FCH],
                                       start=(k == 0), stop=(k == KT - 1)),
                              waits=wl2, inc=(1 if k == KT - 1 else 0))
            t_w1 = mk.last("tensor")
            mk.ws.consumed(t_w1)
            b1a = mk.smallf[:, pi:pi + 1]
            b1g = mk.smallf[:, 22 + pi:23 + pi]
            GEL = mk.ar2[:, 10240:11264]
            for c2 in range(2):
                mk.op("scalar", lambda e, c2=c2, b1g=b1g:
                      e.activation(out=GEL[:, c2 * FCH:(c2 + 1) * FCH],
                                   in_=mk.pb[2 + c2][:, :], func=AF.Gelu,
                                   bias=b1g),
                      waits=[("tensor", t_w1)])
            s_gel = mk.last("scalar")
            hhv = hh_view(mk, pi)
            for c2 in range(2):
                mk.op("vector", lambda e, c2=c2, hhv=hhv, b1a=b1a:
                      e.scalar_tensor_tensor(
                          out=hhv[:, c2 * FCH:(c2 + 1) * FCH],
                          in0=mk.pb[c2][:, :], scalar=b1a,
                          in1=GEL[:, c2 * FCH:(c2 + 1) * FCH],
                          op0=ALU.add, op1=ALU.mult),
                      waits=[("scalar", s_gel)])
        v_hh = mk.last("vector")
        for nt in range(KT):
            wbuf1, cw1 = mk.ws.load(
                [(wdst3(mk.wst[:, 0:2048], 16),
                  wsrc3(w2[li, 0:2048, nt * 128:(nt + 1) * 128]))], 2048)
            for c2 in range(2):
                bank = mk.pb[4 + (nt % 2) * 2 + c2]
                for k in range(16):
                    wl2 = []
                    if k == 0:
                        wl2 = [("scalar", cw1), ("vector", v_hh)]
                        bkey = (4 + (nt % 2) * 2 + c2)
                        if bkey in w2ev:
                            wl2.append(("vector", w2ev[bkey]))
                    mk.op("tensor", lambda e, k=k, c2=c2, bank=bank, tc=tc,
                          wbuf1=wbuf1:
                          e.matmul(out=bank[:, :],
                                   lhsT=wbuf1[:, k * 128:(k + 1) * 128],
                                   rhs=hh_view(mk, k)[:, c2 * FCH:
                                                      (c2 + 1) * FCH],
                                   start=(k == 0), stop=False),
                          waits=wl2,
                          inc=(1 if (c2 == 1 and k == 15) else 0))
            t_p1 = mk.last("tensor")
            mk.ws.consumed(t_p1)
            wbuf2, cw2 = mk.ws.load(
                [(wdst3(mk.wst[:, 0:640], 5),
                  wsrc3(w2[li, 2048:2688, nt * 128:(nt + 1) * 128])),
                 (mk.wst[0:42, 640:768],
                  w2[li, 2688:2730, nt * 128:(nt + 1) * 128])], 768)
            for c2 in range(2):
                bank = mk.pb[4 + (nt % 2) * 2 + c2]
                for k in range(6):
                    kk = 16 + k
                    rows = 128 if k < 5 else 42
                    mk.op("tensor", lambda e, k=k, c2=c2, bank=bank, rows=rows,
                          kk=kk, wbuf2=wbuf2:
                          e.matmul(out=bank[:, :],
                                   lhsT=wbuf2[0:rows, k * 128:k * 128 + 128],
                                   rhs=hh_view(mk, kk)[0:rows,
                                                       c2 * FCH:(c2 + 1) * FCH],
                                   start=False, stop=(k == 5)),
                          waits=[("scalar", cw2)] if k == 0 else (),
                          inc=(1 if k == 5 else 0))
                t_w2 = mk.last("tensor")
                b2c = mk.smallf[:, 44 + nt:45 + nt]
                dst = mk.rk(nt)[:, tok0 + c2 * FCH:tok0 + (c2 + 1) * FCH]
                w2ev[4 + (nt % 2) * 2 + c2] = mk.op(
                    "vector", lambda e, dst=dst, bank=bank, b2c=b2c:
                    e.scalar_tensor_tensor(out=dst, in0=bank[:, :],
                                           scalar=b2c, op0=ALU.add,
                                           in1=dst, op1=ALU.add),
                    waits=[("tensor", t_w2)])
            mk.ws.consumed(mk.last("tensor"))
    return [("vector", mk.last("vector"))]


def hh_view(mk, i):
    if i < 16:
        return mk.arena[:, i * 1024:(i + 1) * 1024]
    return mk.ar2[:, 4096 + (i - 16) * 1024:4096 + (i - 15) * 1024]


def emit_ffn_module(mk, waits, call):
    cd = emit_cast_shift(mk, [mk.rk(k) for k in range(KT)], mk.sq,
                         shifted=False, waits=waits)
    pk = emit_park(mk, list(waits) + [("vector", cd)])
    vd = emit_ln(mk, mk.sq, None, cast_done=cd,
                 extra_wait=[("scalar", pk)],
                 out_tiles_f32=[mk.rk(k) for k in range(KT)])
    w = [("vector", vd)]
    for li in range(mk.fdepth):
        for k in range(KT):
            mk.dump(f"c{call}l{li}_pre{k}", mk.rk(k), w)
        w = ffn_attn_layer(mk, li, w, call)
        for k in range(KT):
            mk.dump(f"c{call}l{li}_pa{k}", mk.rk(k), w)
        w = ffn_ffn_layer(mk, li, w, call)
    vu = emit_unpark_add(mk, w)
    return [("vector", vu)]


def emit_attn_b(mk, waits):
    CCH = 128
    NCC = T // CCH
    ots = mk.dram["ot_scratch"]
    cd = emit_cast_shift(mk, [mk.rk(k) for k in range(KT)], mk.sq,
                         shifted=False, waits=waits)
    pk = emit_park(mk, list(waits) + [("vector", cd)])
    vd = emit_ln(mk, mk.sq, None, cast_done=cd, extra_wait=[("scalar", pk)],
                 out_tiles_f32=[mk.rk(k) for k in range(KT)])
    w = [("vector", vd)]

    LN1G = mk.arena[0:1, 12800:12800 + D]
    LN1B = mk.arena[0:1, 12800 + D:12800 + 2 * D]
    ms = None
    for name, dst in (("ab_ln1_w", LN1G), ("ab_ln1_b", LN1B)):
        ld = mk.dma("sync", mk.strow[0:1, 0:D],
                    mk.dram[name].rearrange("(o f) -> o f", o=1),
                    waits=list(w) if ms is None else [("vector", ms)])
        ms = mk.op("vector", lambda e, dst=dst: e.tensor_copy(
            out=dst, in_=mk.strow[0:1, 0:D]), waits=[("sync", ld)])
    QBC = mk.smallf[:, 60:61]
    KBC = mk.smallf[:, 61:62]
    OBC = mk.smallf[:, 64:72]
    F2BC = mk.smallf[:, 72:80]
    F1BC = mk.smallf[:, 0:32]
    ld = mk.dma("sync", QBC[0:64, :], mk.dram["ab_q_b"].rearrange("(p o) -> p o", o=1),
                waits=[("vector", ms)], allow_slow_non_contiguous=True)
    ld = mk.dma("sync", KBC[0:64, :], mk.dram["ab_k_b"].rearrange("(p o) -> p o", o=1),
                allow_slow_non_contiguous=True)
    ld = mk.dma("sync", OBC, mk.dram["ab_o_b"].rearrange("(f p) -> p f", p=128),
                allow_slow_non_contiguous=True)
    ld = mk.dma("sync", F2BC, mk.dram["ab_f2_b"].rearrange("(f p) -> p f", p=128),
                allow_slow_non_contiguous=True)
    ld = mk.dma("sync", F1BC, mk.dram["ab_f1_b"].rearrange("(f p) -> p f", p=128),
                allow_slow_non_contiguous=True)
    bias_done = [("vector", ms), ("sync", mk.last("sync"))]

    cd1 = emit_cast_shift(mk, [mk.rk(k) for k in range(KT)], mk.sq,
                          shifted=False, waits=list(w))
    v1 = emit_ln(mk, mk.sq, mk.xb, g_row=LN1G, b_row=LN1B, cast_done=cd1,
                 extra_wait=bias_done)
    x_wait = [("vector", v1)]
    VBR = mk.srow16[0:1, 0:D]
    ld = mk.dma("sync", mk.strow[0:1, D:2 * D],
                mk.dram["ab_v_b"].rearrange("(o f) -> o f", o=1),
                waits=list(x_wait) + [("vector", mk.last("vector"))])
    vbm = mk.op("vector", lambda e: e.tensor_copy(out=VBR,
                                                  in_=mk.strow[0:1, D:2 * D]),
                waits=[("sync", ld)])
    x_wait = x_wait + [("vector", vbm)]

    QF = mk.ar2[:, 0:T]
    KF = mk.ar2[:, T:2 * T]
    ES = mk.ar2[:, 2 * T:3 * T].bitcast(F32)
    for qi, (wname, dstb, bcol) in enumerate(
            (("ab_qT", QF, QBC), ("ab_kT", KF, KBC))):
        wap = wsrc3(mk.dram[wname][:, 0:64])
        wbuf, cw = mk.ws.load([(wdst3(mk.wst[:, 0:KT * 64], KT), wap)],
                              KT * 64, extra_wait=list(w))
        for c in range(NCH):
            bank = mk.pb[c % 2]
            for k in range(KT):
                mk.op("tensor", lambda e, k=k, c=c, bank=bank, wbuf=wbuf:
                      e.matmul(out=bank[0:64, :],
                               lhsT=wbuf[:, k * 64:(k + 1) * 64],
                               rhs=mk.bk(mk.xb, k)[:, c * FCH:(c + 1) * FCH],
                               start=(k == 0), stop=(k == KT - 1)),
                      waits=([("scalar", mk.last("scalar")), ("scalar", cw)]
                             + x_wait)
                      if k == 0 else (), inc=(1 if k == KT - 1 else 0))
            t_mm = mk.last("tensor")
            X1 = ES[0:64, 0:FCH]
            X2 = ES[0:64, FCH:2 * FCH]
            vm1 = mk.op("vector", lambda e, bank=bank, bcol=bcol:
                        e.tensor_scalar(out=X1, in0=bank[0:64, :],
                                        scalar1=bcol[0:64, :], scalar2=0.0,
                                        op0=ALU.add, op1=ALU.min),
                        waits=[("tensor", t_mm)])
            se = mk.op("scalar", lambda e: e.activation(out=X1, in_=X1,
                                                        func=AF.Exp),
                       waits=[("vector", vm1)])
            mk.op("vector", lambda e, bank=bank, bcol=bcol:
                  e.tensor_scalar(out=X2, in0=bank[0:64, :], scalar1=bcol[0:64, :],
                                  scalar2=0.0, op0=ALU.add, op1=ALU.max))
            mk.op("vector", lambda e, c=c, dstb=dstb:
                  e.tensor_add(out=dstb[0:64, c * FCH:(c + 1) * FCH],
                               in0=X1, in1=X2), waits=[("scalar", se)])
        mk.ws.consumed(mk.last("tensor"))
    qk_done = mk.last("vector")
    mk.dump("ab_qf", QF[0:64, :], [("vector", qk_done)])
    mk.dump("ab_kf", KF[0:64, :], [("vector", qk_done)])

    onec = cs(mk, CS_ONECOL, 1)
    ones_r = cs(mk, CS_ONEROW, FCH)[0:1, :]

    def v_w_ap(c):
        return wsrc3(mk.dram["ab_vT"][:, c * 256:(c + 1) * 256])

    def v_out(c, tt, bank, t_mm):
        dst = mk.arena[:, tt * 1024 + c * 256:tt * 1024 + (c + 1) * 256]
        m = mk.op("scalar", lambda e, dst=dst, bank=bank:
                  e.activation(out=dst, in_=bank[:, 0:256], func=AF.Copy),
                  waits=[("tensor", t_mm)])
        return ("scalar", m)

    for c in range(4):
        wbuf, cw = mk.ws.load([(wdst3(mk.wst[:, 0:KT * 256], KT), v_w_ap(c))],
                              KT * 256, extra_wait=list(w))
        for tt in range(16):
            bank = mk.pb[2 + (tt % 2)]
            for k in range(KT):
                mk.op("tensor", lambda e, k=k, tt=tt, bank=bank, c=c, wbuf=wbuf:
                      e.matmul(out=bank[:, 0:256],
                               lhsT=mk.bk(mk.xb, k)[:, tt * 128:(tt + 1) * 128],
                               rhs=wbuf[:, k * 256:(k + 1) * 256],
                               start=(k == 0), stop=False),
                      waits=([("scalar", mk.last("scalar")), ("scalar", cw)]
                             + x_wait + [("vector", qk_done)])
                      if k == 0 else ())
            mk.op("tensor", lambda e, bank=bank, c=c:
                  e.matmul(out=bank[:, 0:256], lhsT=ones_r[:, 0:128],
                           rhs=VBR[:, c * 256:(c + 1) * 256],
                           start=False, stop=True), inc=1)
            t_mm = mk.last("tensor")
            if tt == 15:
                mk.ws.consumed(t_mm)
            if c == 0 and tt == 0:
                scr = mk.ar2[:, 3 * T + 512:3 * T + 1024].bitcast(F32)
                sm = mk.op("scalar", lambda e, bank=bank, scr=scr: e.activation(
                    out=scr, in_=bank[:, 0:256], func=AF.Copy),
                    waits=[("tensor", t_mm)])
                mk.dump("ab_pbv", scr, [("scalar", sm)])
            v_out(c, tt, bank, t_mm)
    v_done = mk.last("scalar")
    mk.dump("ab_vtok0", mk.arena[:, 0:1024], [("scalar", v_done)])

    KTOK = mk.ar2[:, 2 * T:2 * T + 1024]
    pbb = mk.pb[4][:, :].bitcast(BF16)
    for tt in range(16):
        tr = mk.op("tensor", lambda e, tt=tt: e.transpose(
            out=pbb[0:128, 0:64], in_=KF[0:64, tt * 128:(tt + 1) * 128],
            identity=mk.ident[0:64, 0:64]),
            waits=[("vector", qk_done)] if tt == 0 else
            [("scalar", mk.last("scalar"))], inc=1)
        mk.op("scalar", lambda e, tt=tt: e.activation(
            out=KTOK[:, tt * 64:(tt + 1) * 64], in_=pbb[0:128, 0:64],
            func=AF.Copy), waits=[("tensor", tr)])
    ktok_done = mk.last("scalar")

    KV16 = mk.ar2[:, 2 * T + 1024:2 * T + 2048]
    KS16 = mk.ar2[0:64, 3 * T:3 * T + 1]
    ATM = mk.ar2[:, 3 * T + 256:3 * T + 384]
    KVF = mk.ar2[:, 6656:8704].bitcast(F32)
    KSF = mk.ar2[:, 8704:8708].bitcast(F32)
    YST = mk.sq
    DENF = mk.strow[0:1, 0:CCH]
    RROW = mk.srow16[0:1, 0:CCH]
    trim = mk.trim
    for cc in range(NCC):
        i0 = cc * CCH
        if cc > 0:
            mk.op("vector", lambda e: e.tensor_copy(out=KV16[0:64, :],
                                                    in_=KVF[0:64, :]))
            mk.op("vector", lambda e: e.tensor_copy(out=KS16, in_=KSF[0:64, 0:1]),
                  inc=1)
        v_state = mk.last("vector")
        t_at = mk.op("tensor", lambda e, i0=i0: e.matmul(
            out=mk.pb[0][:, 0:CCH], lhsT=KF[0:64, i0:i0 + CCH],
            rhs=QF[0:64, i0:i0 + CCH], start=True, stop=True),
            waits=[("vector", v_state), ("scalar", ktok_done),
                   ("scalar", mk.last("scalar"))], inc=1)
        v_atm = mk.op("vector", lambda e: e.tensor_tensor(
            out=ATM, in0=mk.pb[0][:, 0:CCH], in1=trim[:, :], op=ALU.mult),
            waits=[("tensor", t_at)])
        if cc > 0:
            mk.op("tensor", lambda e, i0=i0: e.matmul(
                out=mk.pb[4][0:1, 0:CCH], lhsT=KS16[:, :],
                rhs=QF[0:64, i0:i0 + CCH], start=True, stop=False),
                waits=[("vector", v_atm)])
        t_den = mk.op("tensor", lambda e, cc=cc: e.matmul(
            out=mk.pb[4][0:1, 0:CCH], lhsT=onec[0:128, :], rhs=ATM,
            start=(cc == 0), stop=True),
            waits=[("vector", v_atm)] if cc == 0 else (), inc=1)
        mk.op("vector", lambda e: e.tensor_scalar_add(
            out=DENF, in0=mk.pb[4][0:1, 0:CCH], scalar1=1e-6),
            waits=[("tensor", t_den)])
        mk.op("vector", lambda e: e.reciprocal(out=DENF, in_=DENF))
        v_rec = mk.op("vector", lambda e: e.tensor_copy(out=RROW, in_=DENF))
        t_rb = mk.op("tensor", lambda e: e.matmul(
            out=mk.pb[4][:, 0:CCH], lhsT=ones_r[:, 0:128], rhs=RROW,
            start=True, stop=True), waits=[("vector", v_rec)], inc=1)
        v_a2 = mk.op("vector", lambda e: e.tensor_tensor(
            out=ATM, in0=ATM, in1=mk.pb[4][:, 0:CCH], op=ALU.mult),
            waits=[("tensor", t_rb)])
        Q2 = mk.ar2[0:64, 3 * T + 384:3 * T + 512]
        v_q2 = mk.op("vector", lambda e, i0=i0: e.tensor_tensor(
            out=Q2, in0=QF[0:64, i0:i0 + CCH], in1=mk.pb[4][0:64, 0:CCH],
            op=ALU.mult))
        for dvt in range(KT):
            wl = [("vector", v_q2)]
            if dvt >= 2:
                wl.append(("scalar", mk.last("scalar")))
            if cc > 0:
                mk.op("tensor", lambda e, dvt=dvt: e.matmul(
                    out=mk.pb[1 + (dvt % 2)][:, 0:CCH],
                    lhsT=KV16[0:64, dvt * 128:(dvt + 1) * 128],
                    rhs=Q2, start=True, stop=False), waits=wl)
            mk.op("tensor", lambda e, dvt=dvt, cc=cc: e.matmul(
                out=mk.pb[1 + (dvt % 2)][:, 0:CCH],
                lhsT=mk.arena[:, cc * 1024 + dvt * 128:cc * 1024 + (dvt + 1) * 128],
                rhs=ATM, start=(cc == 0), stop=True),
                waits=wl if cc == 0 else (), inc=1)
            t_n = mk.last("tensor")
            mk.op("scalar", lambda e, dvt=dvt: e.activation(
                out=YST[:, dvt * 128:(dvt + 1) * 128],
                in_=mk.pb[1 + (dvt % 2)][:, 0:CCH], func=AF.Copy),
                waits=[("tensor", t_n)])
        s_y = mk.last("scalar")
        if cc == 0:
            mk.dump("ab_yst0", YST[:, 0:1024], [("scalar", s_y)])
        mk.dma("scalar", ots.rearrange("(dvt p) f -> p dvt f", p=128)
               [:, :, i0:i0 + CCH], YST[:, 0:1024].rearrange(
                   "p (dvt f) -> p dvt f", dvt=8), waits=[("scalar", s_y)])
        t_u1 = mk.op("tensor", lambda e, cc=cc: e.matmul(
            out=mk.pb[6][0:64, :], lhsT=KTOK[:, cc * 64:(cc + 1) * 64],
            rhs=mk.arena[:, cc * 1024:cc * 1024 + 512],
            start=True, stop=True), waits=[("scalar", s_y)])
        t_u2 = mk.op("tensor", lambda e, cc=cc: e.matmul(
            out=mk.pb[7][0:64, :], lhsT=KTOK[:, cc * 64:(cc + 1) * 64],
            rhs=mk.arena[:, cc * 1024 + 512:cc * 1024 + 1024],
            start=True, stop=True))
        t_u3 = mk.op("tensor", lambda e, cc=cc: e.matmul(
            out=mk.pb[5][0:64, 0:1], lhsT=KTOK[:, cc * 64:(cc + 1) * 64],
            rhs=cs(mk, CS_ONECOL, 1),
            start=True, stop=True), inc=1)
        t_u = mk.last("tensor")
        if cc == 0:
            mk.op("vector", lambda e: e.tensor_copy(out=KVF[0:64, 0:512],
                                                    in_=mk.pb[6][0:64, :]),
                  waits=[("tensor", t_u)])
            mk.op("vector", lambda e: e.tensor_copy(out=KVF[0:64, 512:1024],
                                                    in_=mk.pb[7][0:64, :]))
            mk.op("vector", lambda e: e.tensor_copy(out=KSF[0:64, 0:1],
                                                    in_=mk.pb[5][0:64, 0:1]),
                  inc=1)
        else:
            mk.op("vector", lambda e: e.tensor_add(out=KVF[0:64, 0:512],
                                                   in0=KVF[0:64, 0:512],
                                                   in1=mk.pb[6][0:64, :]),
                  waits=[("tensor", t_u)])
            mk.op("vector", lambda e: e.tensor_add(out=KVF[0:64, 512:1024],
                                                   in0=KVF[0:64, 512:1024],
                                                   in1=mk.pb[7][0:64, :]))
            mk.op("vector", lambda e: e.tensor_add(out=KSF[0:64, 0:1],
                                                   in0=KSF[0:64, 0:1],
                                                   in1=mk.pb[5][0:64, 0:1]),
                  inc=1)
        la_done = mk.last("tensor")

    for k in range(KT):
        mk.dma("sync", mk.sq[:, k * 2048:(k + 1) * 2048],
               ots[k * 128:(k + 1) * 128, :],
               waits=[("tensor", la_done), ("scalar", mk.last("scalar"))]
               if k == 0 else ())
    ot_ld = mk.last("sync")

    def o_ap(nt):
        return wsrc3(mk.dram["ab_oT"][:, nt * 128:(nt + 1) * 128])

    def o_out(nt, c, bank, t_mm):
        dst = mk.rk(nt)[:, c * FCH:(c + 1) * FCH]
        bcol = OBC[:, nt:nt + 1]
        m1_ = mk.op("vector", lambda e, dst=dst, bank=bank, bcol=bcol:
                    e.scalar_tensor_tensor(out=dst, in0=bank[:, :], scalar=bcol,
                                           op0=ALU.add, in1=dst, op1=ALU.add),
                    waits=[("tensor", t_mm)])
        return ("vector", m1_)

    emit_gemm_b(mk, o_ap, 8, lambda k: mk.sq[:, k * 2048:(k + 1) * 2048],
                o_out, x_wait=[("sync", ot_ld),
                               ("vector", mk.last("vector"))],
                cast_eng="scalar")
    w = [("vector", mk.last("vector"))]
    for k in range(KT):
        mk.dump(f"ab_post_o{k}", mk.rk(k), w)

    LN2G = mk.arena[0:1, 12800:12800 + D]
    LN2B = mk.arena[0:1, 12800 + D:12800 + 2 * D]
    ms2 = None
    for name, dst in (("ab_ln2_w", LN2G), ("ab_ln2_b", LN2B)):
        ld = mk.dma("sync", mk.strow[0:1, 0:D],
                    mk.dram[name].rearrange("(o f) -> o f", o=1),
                    waits=(list(w) if ms2 is None else [("vector", ms2)]) +
                    [("vector", mk.last("vector"))])
        ms2 = mk.op("vector", lambda e, dst=dst: e.tensor_copy(
            out=dst, in_=mk.strow[0:1, 0:D]), waits=[("sync", ld)])
    cd2 = emit_cast_shift(mk, [mk.rk(k) for k in range(KT)], mk.sq,
                          shifted=False, waits=w)
    v2 = emit_ln(mk, mk.sq, mk.xb, g_row=LN2G, b_row=LN2B, cast_done=cd2,
                 extra_wait=[("vector", ms2)])
    x_wait = [("vector", v2)]
    f1 = mk.dram["ab_f1T"]
    f2 = mk.dram["ab_f2T"]
    for tch in range(4):
        tok0 = tch * FCH
        for nt in range(32):
            wbuf, cw = mk.ws.load(
                [(wdst3(mk.wst[:, 0:1024], KT),
                  wsrc3(f1[:, nt * 128:(nt + 1) * 128]))], 1024,
                extra_wait=list(w))
            bank = mk.pb[nt % 2]
            for k in range(KT):
                mk.op("tensor", lambda e, k=k, nt=nt, bank=bank, tok0=tok0,
                      wbuf=wbuf:
                      e.matmul(out=bank[:, :],
                               lhsT=wbuf[:, k * 128:(k + 1) * 128],
                               rhs=mk.bk(mk.xb, k)[:, tok0:tok0 + FCH],
                               start=(k == 0), stop=(k == KT - 1)),
                      waits=([("scalar", mk.last("scalar")), ("scalar", cw)]
                             + x_wait)
                      if k == 0 else (), inc=(1 if k == KT - 1 else 0))
            t_mm = mk.last("tensor")
            mk.ws.consumed(t_mm)
            mk.op("scalar", lambda e, nt=nt, bank=bank: e.activation(
                out=mk.arena[:, nt * FCH:(nt + 1) * FCH], in_=bank[:, :],
                func=AF.Gelu, bias=F1BC[:, nt:nt + 1]),
                waits=[("tensor", t_mm)])
        h_done = mk.last("scalar")
        for nt in range(KT):
            bank = mk.pb[4 + (nt % 2)]
            for piece in range(2):
                wbuf, cw = mk.ws.load(
                    [(wdst3(mk.wst[:, 0:2048], 16),
                      wsrc3(f2[piece * 2048:(piece + 1) * 2048,
                               nt * 128:(nt + 1) * 128]))], 2048)
                for k in range(16):
                    kk = piece * 16 + k
                    mk.op("tensor", lambda e, k=k, kk=kk, bank=bank, wbuf=wbuf:
                          e.matmul(out=bank[:, :],
                                   lhsT=wbuf[:, k * 128:(k + 1) * 128],
                                   rhs=mk.arena[:, kk * FCH:(kk + 1) * FCH],
                                   start=(kk == 0), stop=(kk == 31)),
                          waits=[("scalar", cw), ("scalar", h_done),
                                 ("vector", mk.last("vector"))]
                          if k == 0 else (), inc=(1 if k == 15 else 0))
                mk.ws.consumed(mk.last("tensor"))
            t_mm = mk.last("tensor")
            dst = mk.rk(nt)[:, tok0:tok0 + FCH]
            bcol = F2BC[:, nt:nt + 1]
            mk.op("vector", lambda e, dst=dst, bank=bank, bcol=bcol:
                  e.scalar_tensor_tensor(out=dst, in0=bank[:, :], scalar=bcol,
                                         op0=ALU.add, in1=dst, op1=ALU.add),
                  waits=[("tensor", t_mm)])
    w = [("vector", mk.last("vector"))]
    vu = emit_unpark_add(mk, w)
    return [("vector", vu)]



def host_consts():
    import ml_dtypes
    bf = lambda a: np.ascontiguousarray(a.astype(ml_dtypes.bfloat16))
    c = {}
    c["c_ident16"] = bf(np.eye(128))
    small = np.zeros((128, 960), np.float64)
    small[:, CS_ONECOL] = 1.0
    small[0:64, CS_SEL2] = 1.0
    small[64:128, CS_SEL2 + 1] = 1.0
    for p in range(128):
        small[p // 64, CS_C2T + p] = 1.0
        small[p % 64, CS_KBSEL + p] = 1.0
        small[(p + 32) % 64, CS_KBSELP + p] = 1.0
    small[0:1, CS_ONEROW:CS_ONEROW + FCH] = 1.0
    c["c_small16"] = bf(small)

    masks = np.zeros((128, 1664), np.float64)
    for p in range(128):
        for g in range(640):
            d = 384 + p - g
            if not (0 <= d <= 256):
                masks[p, g] = NEG
    for p in range(128):
        for g in range(384):
            if not (p >= g - 128):
                masks[p, 640 + g] = NEG
    for p in range(128):
        for g in range(640):
            d = 256 + (g - 384) - p
            if not (0 <= d <= 256):
                masks[p, 1024 + g] = NEG
    c["c_masks16"] = bf(masks)

    perm = np.zeros((128, 128), np.float64)
    for a in range(128):
        g, r = a // 64, a % 64
        psrc = g * 64 + ((r + 32) % 64)
        perm[psrc, a] = 1.0
    c["c_perm16"] = bf(perm)
    vbsel = np.zeros((128, 4096), np.float64)
    for t in range(32):
        for p in range(128):
            dd = 2 * t + p // 64
            vbsel[dd, t * 128 + p] = 1.0
    vbsel[64:128] = vbsel[0:64]
    c["c_vbsel16"] = bf(vbsel)
    msum = np.zeros((128, 2048), np.float64)
    for t in range(32):
        for p in range(128):
            msum[p, t * 64 + 2 * t + p // 64] = 1.0
    c["c_msum16"] = bf(msum)
    kbr = np.zeros((128, 128), np.float64)
    for p in range(128):
        kbr[p % 64, p] = 1.0
        kbr[64 + p % 64, p] = 1.0
    c["c_kbselr16"] = bf(kbr)
    tri = np.zeros((128, 128), np.float64)
    for s in range(128):
        tri[s, s:] = 1.0
    c["c_trim16"] = bf(tri)

    half = np.arange(0, DK, 2, dtype=np.float64)
    inv_freq = 1.0 / 10000.0 ** (half / DK)
    t = np.arange(T, dtype=np.float64)
    freqs = t[:, None] * inv_freq[None, :]
    freqs = np.concatenate([freqs, freqs], -1)
    base_scale = (half + 0.4 * DK) / (1.4 * DK)
    power = (t - T // 2) / (WIN // 2)
    scale = base_scale[None, :] ** power[:, None]
    scale = np.concatenate([scale, scale], -1)
    cosf, sinf = np.cos(freqs), np.sin(freqs)
    sgn = np.ones((T, 64))
    sgn[:, :32] = -1.0
    csq = (cosf * scale).T
    ssq = (sgn * sinf * scale).T
    csk = (cosf / scale).T
    ssk = (sgn * sinf / scale).T
    tab = np.concatenate([csq, ssq, csk, ssk], axis=1)
    tab = np.concatenate([tab, tab], axis=0)
    c["c_tab16"] = bf(tab)

    fsmall = np.zeros((128, 8), np.float32)
    fsmall[:, 0] = LN_EPS
    fsmall[:, 1] = 1.0
    fsmall[:, 2] = 1e-24
    fsmall[:, 3] = 1e-6
    c["c_fsmall"] = fsmall
    return c



def prepare_in_maps(mk, ins, ncore=8):
    x = np.ascontiguousarray(ins["L0_attn_a_x"])
    wmap = {}
    wmap.update(host_consts())
    wmap["aa_wqkvT"] = np.ascontiguousarray(ins["L0_attn_a_W_qkv"].T)
    wmap["aa_woT"] = np.ascontiguousarray(ins["L0_attn_a_W_o"].T)
    wmap["aa_wltm"] = np.ascontiguousarray(ins["L0_attn_a_W_LTM"])
    wmap["aa_wltmT"] = np.ascontiguousarray(
        ins["L0_attn_a_W_LTM"].transpose(0, 2, 1))
    for nm in ("V_gs", "V_T0", "beta_tau", "beta_gm", "C_ch", "gamma",
               "alpha_ppd", "IC_threshold"):
        wmap["aa_" + nm] = np.ascontiguousarray(ins["L0_attn_a_" + nm])
    wmap["ab_qT"] = np.ascontiguousarray(ins["L0_attn_b_q_w"].T)
    wmap["ab_kT"] = np.ascontiguousarray(ins["L0_attn_b_k_w"].T)
    wmap["ab_vT"] = np.ascontiguousarray(ins["L0_attn_b_v_w"].T)
    wmap["ab_oT"] = np.ascontiguousarray(ins["L0_attn_b_o_w"].T)
    wmap["ab_f1T"] = np.ascontiguousarray(ins["L0_attn_b_f1_w"].T)
    wmap["ab_f2T"] = np.ascontiguousarray(ins["L0_attn_b_f2_w"].T)
    for nm in ("ln1_w", "ln1_b", "ln2_w", "ln2_b", "q_b", "k_b", "v_b",
               "o_b", "f1_b", "f2_b"):
        wmap["ab_" + nm] = np.ascontiguousarray(ins["L0_attn_b_" + nm])
    for nm in ("Wqkv", "q_scale", "k_scale", "Wo", "ln_g", "ln_b", "W1",
               "b1", "W2", "b2"):
        wmap["ff_" + nm] = np.ascontiguousarray(ins["L0_ffn_" + nm])

    in_maps = []
    for ci in range(ncore):
        m = {}
        for name in mk.dram:
            if name == "x_t":
                m[name] = np.ascontiguousarray(x[ci].T)
            elif name in wmap:
                m[name] = wmap[name]
            elif name in mk.consts:
                m[name] = mk.consts[name]
        in_maps.append(m)
    return in_maps


def kernel(**ins):
    ncore = 8
    mk = build_program(debug_dump=tuple(DEBUG["dump"]), blocks=DEBUG["blocks"],
                       fdepth=DEBUG["fdepth"])
    in_maps = prepare_in_maps(mk, ins, ncore)
    nc = mk.build()
    res = run_bass_kernel_spmd(nc, in_maps, core_ids=list(range(ncore)))
    outs = []
    for ci in range(ncore):
        outs.append(np.ascontiguousarray(res.results[ci]["out_t"].T))
    out = np.stack(outs, axis=0).astype(np.float32)
    if mk.dumped:
        kernel._dbg = [{d: res.results[ci][d] for d in mk.dumped}
                       for ci in range(ncore)]
    return out


# version 10 (final)
# speedup vs baseline: 1.2275x; 1.2275x over previous
import numpy as np
from contextlib import ExitStack

import concourse.bass as bass
import concourse.mybir as mybir
from concourse.bass_utils import run_bass_kernel_spmd

F32 = mybir.dt.float32
BF16 = mybir.dt.bfloat16
AF = mybir.ActivationFunctionType
ALU = mybir.AluOpType
AX = mybir.AxisListType

D = 1024
T = 2048
KT = 8
NH = 16
DK = 64
WIN = 256
NWIN = T // WIN
FCH = 512
NCH = T // FCH
DI = 2730
FDEPTH = 6
LN_EPS = 1e-5
NEG = -1e30

ENGS = ("sync", "gpsimd", "tensor", "vector", "scalar")

DEBUG = {"dump": [], "blocks": None, "fdepth": FDEPTH}


class WStream:

    def __init__(self, mk, cast_eng="scalar"):
        self.mk = mk
        self.i = 0
        self.prev_cast = None
        self.cons = [None, None]

    def load(self, jobs, cols, extra_wait=()):
        mk = self.mk
        buf = self.i % 2
        wl = list(extra_wait)
        if self.prev_cast is not None:
            wl.append(("scalar", self.prev_cast))
        first = True
        for dst, srcap in jobs:
            mk.dma("sync", dst, srcap, waits=wl if first else ())
            first = False
        ld = mk.last("sync")
        wlc = [("sync", ld)]
        if self.cons[buf] is not None:
            wlc.append(("tensor", self.cons[buf]))
        cd = mk.op("scalar", lambda e, buf=buf, cols=cols: e.copy(
            out=mk.wbf[buf][:, 0:cols], in_=mk.wst[:, 0:cols]),
            waits=wlc)
        self.prev_cast = cd
        self.i += 1
        self._last_buf = buf
        return mk.wbf[buf], cd

    def consumed(self, t_ms):
        self.cons[self._last_buf] = t_ms


class MK:
    def __init__(self, debug_dump=(), blocks=None, fdepth=FDEPTH):
        self.nc = bass.Bass("TRN2", target_bir_lowering=False, debug=False,
                            dynamic_dma_scratch_size=512)
        self.es = ExitStack()
        self.q = {e: [] for e in ENGS}
        self.c = {e: 0 for e in ENGS}
        self.sems = {e: self.es.enter_context(self.nc.semaphore("sem_" + e))
                     for e in ENGS}

        self.consts = {}
        self.dram = {}
        self.debug_dump = set(debug_dump)
        self.blocks = set(blocks) if blocks is not None else {
            "attn_a", "ffn1", "attn_b", "ffn2"}
        self.fdepth = fdepth
        self.dumped = []
        self._alloc()

    VSG = ("vector", "scalar", "gpsimd")

    def op(self, eng, fn, waits=(), inc=0, is_dma=False):
        selfwait = -1
        if eng in self.VSG:
            selfwait = self.c[eng]
            if not is_dma:
                inc = 1
        if is_dma:
            selfwait = self.c[eng]
        if inc:
            self.c[eng] += inc
        self.q[eng].append((fn, tuple(waits), inc, self.c[eng], selfwait))
        return self.c[eng]

    def last(self, eng):
        return self.c[eng]

    def dma(self, eng, out_ap, in_ap, waits=(), **kw):
        return self.op(eng, lambda e: e.dma_start(out=out_ap, in_=in_ap, **kw),
                       waits=waits, inc=16, is_dma=True)

    def build(self):
        nc = self.nc
        sems = self.sems
        with nc.Block() as blk:
            for e in ENGS:
                ops = self.q[e]

                def make_body(ops=ops, myeng=e):
                    def body(engobj):
                        waited = {}
                        owaited = -1
                        for fn, waits, inc, val, selfwait in ops:
                            for (we, wv) in waits:
                                if we == myeng:
                                    continue
                                if waited.get(we, -1) < wv:
                                    engobj.wait_ge(sems[we], wv)
                                    waited[we] = wv
                            if selfwait > 0 and selfwait > owaited:
                                engobj.wait_ge(sems[myeng], selfwait)
                                owaited = selfwait
                            ins = fn(engobj)
                            if inc:
                                ins.then_inc(sems[myeng], inc)
                    return body
                getattr(blk, e)(make_body())
        return nc

    def sbt(self, name, shape, dtype):
        return self.es.enter_context(self.nc.sbuf_tensor(name, shape, dtype))

    def pst(self, name, shape, dtype=F32):
        return self.es.enter_context(self.nc.psum_tensor(name, shape, dtype))

    def hbm(self, name, shape, dtype=F32, kind="ExternalInput"):
        ap = self.nc.dram_tensor(name, list(shape), dtype, kind=kind).ap()
        self.dram[name] = ap
        return ap

    def const(self, name, arr):
        arr = np.ascontiguousarray(arr.astype(np.float32))
        self.consts[name] = arr
        return self.hbm(name, arr.shape, F32)

    def dump(self, name, sbuf_ap, waits):
        if name not in self.debug_dump:
            return
        out = self.hbm("dbg_" + name, list(sbuf_ap.shape), sbuf_ap.dtype,
                       kind="ExternalOutput")
        self.dumped.append("dbg_" + name)
        d = self.dma("scalar", out, sbuf_ap, waits=waits)
        self.op("vector", lambda e: e.engine_nop(), waits=[("scalar", d)])
        self.op("tensor", lambda e: e.nop(), waits=[("scalar", d)])

    def _alloc(self):
        self.resid = self.sbt("resid", [128, KT * T], F32)
        self.xb = self.sbt("xb", [128, KT * T], BF16)
        self.sq = self.sbt("sq", [128, KT * T], BF16)
        self.arena = self.sbt("arena", [128, 16 * 1024], BF16)
        self.ar2 = self.sbt("ar2", [128, 12 * 1024], BF16)
        self.wst = self.sbt("wst", [128, 2048], F32)
        self.wbf = [self.sbt("wbf0", [128, 2048], BF16),
                    self.sbt("wbf1", [128, 2048], BF16)]
        self.strow = self.sbt("strow", [128, T], F32)
        self.srow16 = self.sbt("srow16", [128, 2 * T], BF16)
        self.smallf = self.sbt("smallf", [128, 128], F32)
        self.ident = self.sbt("ident", [128, 128], BF16)
        self.csmall = self.sbt("csmall", [128, 960], BF16)
        self.perm = self.sbt("perm", [128, 128], BF16)
        self.trim = self.sbt("trim", [128, 128], BF16)
        self.masks = self.sbt("masks", [128, 1664], BF16)
        self.pbig = self.pst("pbig", [128, 8 * FCH], F32)
        self.pb = [self.pbig[:, i * FCH:(i + 1) * FCH] for i in range(8)]

    def rk(self, k):
        return self.resid[:, k * T:(k + 1) * T]

    def bk(self, buf, k):
        return buf[:, k * T:(k + 1) * T]


CS_ONECOL = 0
CS_SEL2 = 8
CS_C2T = 12
CS_KBSEL = 140
CS_ONEROW = 268
CS_KBSELP = 780


def cs(mk, off, n):
    return mk.csmall[:, off:off + n]


def m1(mk, it):
    return mk.masks[:, 128 - 128 * it:640 - 128 * it]


def m1w0(mk, it):
    return mk.masks[:, 640 + 128 - 128 * it:640 + 384 - 128 * it]


def m2(mk, jt):
    return mk.masks[:, 1024 + 384 - 128 * jt:1024 + 640 - 128 * jt]



def wsrc3(sl, p=128):
    return sl.rearrange("(kt p) f -> p kt f", p=p)


def wdst3(stage_slice, kt):
    return stage_slice.rearrange("p (kt f) -> p kt f", kt=kt)


def emit_cast_shift(mk, src_tiles, out_buf, shifted, waits=()):
    for k in range(KT):
        src = src_tiles[k]
        dst = mk.bk(out_buf, k)
        wl = waits if k == 0 else ()
        if shifted and k >= 4:
            mk.op("vector", lambda e, d=dst: e.memset(d[:, 0:1], 0.0), waits=wl)
            mk.op("vector", lambda e, s=src, d=dst:
                  e.tensor_copy(out=d[:, 1:T], in_=s[:, 0:T - 1]))
        else:
            mk.op("vector", lambda e, s=src, d=dst:
                  e.tensor_copy(out=d, in_=s), waits=wl)
    return mk.op("vector", lambda e: e.engine_nop(), inc=1)


def emit_ln(mk, raw_buf, out_buf, g_row=None, b_row=None, cast_done=None,
            extra_wait=(), out_tiles_f32=None):
    assert out_buf is None or out_buf is not raw_buf

    def sqk(k):
        if k < 6:
            return mk.ar2[:, k * T:(k + 1) * T]
        return mk.arena[:, (k - 6) * T:(k - 5) * T]

    wl0 = list(extra_wait)
    if cast_done is not None:
        wl0.append(("vector", cast_done))
    for k in range(KT):
        mk.op("scalar", lambda e, k=k:
              e.activation(out=sqk(k), in_=mk.bk(raw_buf, k), func=AF.Square),
              waits=wl0 if k == 0 else (), inc=(1 if k == KT - 1 else 0))
    sq_done = mk.last("scalar")

    onec = cs(mk, CS_ONECOL, 1)
    arF0 = mk.arena[:].bitcast(F32)
    ev = []
    for c in range(NCH):
        bank_s = mk.pb[6]
        bank_q = mk.pb[7]
        wl = [("scalar", sq_done)] + list(extra_wait)
        if cast_done is not None:
            wl.append(("vector", cast_done))
        if c >= 1:
            wl.append(("scalar", ev[c - 1]))
        for k in range(KT):
            mk.op("tensor", lambda e, k=k, c=c, bank_s=bank_s:
                  e.matmul(out=bank_s[0:1, :], lhsT=onec,
                           rhs=mk.bk(raw_buf, k)[:, c * FCH:(c + 1) * FCH],
                           start=(k == 0), stop=(k == KT - 1)),
                  waits=wl if k == 0 else ())
            mk.op("tensor", lambda e, k=k, c=c, bank_q=bank_q:
                  e.matmul(out=bank_q[0:1, :], lhsT=onec,
                           rhs=sqk(k)[:, c * FCH:(c + 1) * FCH],
                           start=(k == 0), stop=(k == KT - 1)),
                  inc=(1 if k == KT - 1 else 0))
        t_mm = mk.last("tensor")
        mk.op("scalar", lambda e, c=c, bank_s=bank_s:
              e.activation(out=mk.strow[0:1, c * FCH:(c + 1) * FCH],
                           in_=bank_s[0:1, :], func=AF.Copy, scale=1.0 / D),
              waits=[("tensor", t_mm)])
        e_ev = mk.op("scalar", lambda e, c=c, bank_q=bank_q:
                     e.activation(out=arF0[0:1, 2304 + c * FCH:2304 + (c + 1) * FCH],
                                  in_=bank_q[0:1, :], func=AF.Copy, scale=1.0 / D),
                     inc=1)
        ev.append(e_ev)
    s_stats = mk.last("scalar")
    arF = mk.arena[:].bitcast(F32)
    MU = mk.strow[0:1, 0:T]
    MSQ = arF[0:1, 2304:2304 + T]
    VAR = arF[0:1, 4352:4352 + T]
    mk.op("vector", lambda e: e.tensor_mul(out=VAR, in0=MU, in1=MU),
          waits=[("scalar", s_stats)])
    v_var = mk.op("vector", lambda e: e.tensor_sub(out=VAR, in0=MSQ, in1=VAR),
                  inc=1)
    LNV = MSQ
    mk.op("scalar", lambda e: e.activation(out=LNV, in_=VAR, func=AF.Ln,
                                           bias=mk.smallf[0:1, 110:111]),
          waits=[("vector", v_var)])
    RSTD = mk.srow16[0:1, 0:T]
    s_rstd = mk.op("scalar", lambda e: e.activation(out=RSTD, in_=LNV,
                                                    func=AF.Exp, scale=-0.5),
                   inc=1)
    NMR = mk.srow16[0:1, T:2 * T]
    v_rows = mk.op("vector", lambda e: e.scalar_tensor_tensor(
        out=NMR, in0=MU, scalar=-1.0, in1=RSTD, op0=ALU.mult, op1=ALU.mult),
        waits=[("scalar", s_rstd)], inc=1)

    ones_r = cs(mk, CS_ONEROW, FCH)[0:1, :]
    t1 = mk.arena[:, 2 * T:2 * T + FCH]
    prev = [None, None]
    for c in range(NCH):
        for k in range(KT):
            idx = (c * KT + k) % 2
            bank_s = mk.pb[4 if idx == 0 else 2]
            bank_t = mk.pb[5 if idx == 0 else 3]
            lg = g_row[:, k * 128:(k + 1) * 128] if g_row is not None \
                else ones_r[:, 0:128]
            wl = [("vector", v_rows)]
            if prev[idx] is not None:
                wl.append(("vector", prev[idx]))
            mk.op("tensor", lambda e, lg=lg, c=c, bank_s=bank_s:
                  e.matmul(out=bank_s[:, :], lhsT=lg,
                           rhs=RSTD[:, c * FCH:(c + 1) * FCH],
                           start=True, stop=True),
                  waits=wl)
            mk.op("tensor", lambda e, lg=lg, c=c, bank_t=bank_t:
                  e.matmul(out=bank_t[:, :], lhsT=lg,
                           rhs=NMR[:, c * FCH:(c + 1) * FCH],
                           start=True, stop=(b_row is None)),
                  inc=(0 if b_row is not None else 1))
            if b_row is not None:
                br = b_row[:, k * 128:(k + 1) * 128]
                mk.op("tensor", lambda e, br=br, bank_t=bank_t:
                      e.matmul(out=bank_t[:, :], lhsT=br, rhs=ones_r,
                               start=False, stop=True), inc=1)
            t_bc = mk.last("tensor")
            rawsl = mk.bk(raw_buf, k)[:, c * FCH:(c + 1) * FCH]
            if out_buf is not None:
                outsl = mk.bk(out_buf, k)[:, c * FCH:(c + 1) * FCH]
            else:
                outsl = out_tiles_f32[k][:, c * FCH:(c + 1) * FCH]
            mk.op("vector", lambda e, rawsl=rawsl, bank_s=bank_s:
                  e.tensor_tensor(out=t1, in0=rawsl, in1=bank_s[:, :], op=ALU.mult),
                  waits=[("tensor", t_bc)])
            prev[idx] = mk.op("vector", lambda e, outsl=outsl, bank_t=bank_t:
                              e.tensor_tensor(out=outsl, in0=t1,
                                              in1=bank_t[:, :], op=ALU.add), inc=1)
    return mk.last("vector")


def emit_gemm_b(mk, w_col_ap, n_tiles, xsrc, out_cb, n0=0, cast_eng="vector",
                x_wait=(), tag=""):
    evac = {}
    for nt in range(n_tiles):
        wbuf, cd = mk.ws.load([(wdst3(mk.wst[:, 0:KT * 128], KT),
                                w_col_ap(nt))], KT * 128,
                              extra_wait=list(x_wait))
        for c in range(NCH):
            slot = (nt * NCH + c) % 4
            bank = mk.pb[slot]
            wl = [("scalar", cd)] + list(x_wait)
            if (nt * NCH + c) >= 4:
                wl.append(evac[(nt * NCH + c) - 4])
            for k in range(KT):
                last = (k == KT - 1)
                mk.op("tensor", lambda e, k=k, c=c, bank=bank, last=last,
                      wbuf=wbuf:
                      e.matmul(out=bank[:, :],
                               lhsT=wbuf[:, k * 128:(k + 1) * 128],
                               rhs=xsrc(k)[:, c * FCH:(c + 1) * FCH],
                               start=(k == 0), stop=last),
                      waits=wl if k == 0 else (), inc=(1 if last else 0))
            t_mm = mk.last("tensor")
            evac[nt * NCH + c] = out_cb(n0 + nt, c, bank, t_mm)
        mk.ws.consumed(mk.last("tensor"))
    return


def emit_gemm_a(mk, w_row_ap, n_chunks, nch_cols, xsrc, out_cb, x_wait=(),
                cast_eng="scalar", kt=KT, tag=""):
    evac = {}
    TT = T // 128
    for c in range(n_chunks):
        wbuf, cd = mk.ws.load([(wdst3(mk.wst[:, 0:kt * nch_cols], kt),
                                w_row_ap(c))], kt * nch_cols,
                              extra_wait=list(x_wait))
        for tt in range(TT):
            slot = (c * TT + tt) % 4
            bank = mk.pb[slot]
            wl = [("scalar", cd)] + list(x_wait)
            if (c * TT + tt) >= 4:
                wl.append(evac[(c * TT + tt) - 4])
            for k in range(kt):
                last = (k == kt - 1)
                mk.op("tensor", lambda e, k=k, tt=tt, bank=bank, last=last,
                      wbuf=wbuf:
                      e.matmul(out=bank[:, 0:nch_cols],
                               lhsT=xsrc(k)[:, tt * 128:(tt + 1) * 128],
                               rhs=wbuf[:, k * nch_cols:(k + 1) * nch_cols],
                               start=(k == 0), stop=last),
                      waits=wl if k == 0 else (), inc=(1 if last else 0))
            t_mm = mk.last("tensor")
            evac[c * TT + tt] = out_cb(c, tt, bank, t_mm)
        mk.ws.consumed(mk.last("tensor"))
    return


def wcol_b(mk, name, n_off, ncols=128):
    ap = mk.dram[name]

    def f(nt):
        off = n_off + nt * ncols
        sl = ap[:, off:off + ncols]
        return sl.rearrange("(kt p) f -> p (kt f)", p=128)
    return f


def emit_resid_add(mk, nt, c, bank, t_mm, extra=None, scale_ap=None):
    dst = mk.rk(nt)[:, c * FCH:(c + 1) * FCH]
    if extra is None:
        m = mk.op("vector", lambda e, dst=dst, bank=bank:
                  e.tensor_tensor(out=dst, in0=dst, in1=bank[:, :], op=ALU.add),
                  waits=[("tensor", t_mm)], inc=1)
    else:
        raise NotImplementedError
    return ("vector", m)



def emit_load_input(mk):
    xin = mk.hbm("x_t", [D, T], F32)
    for k in range(KT):
        mk.dma("scalar", mk.rk(k), xin[k * 128:(k + 1) * 128, :])
    return mk.last("scalar")


def emit_store_output(mk, waits):
    out = mk.hbm("out_t", [D, T], F32, kind="ExternalOutput")
    for k in range(KT):
        mk.dma("scalar", out[k * 128:(k + 1) * 128, :], mk.rk(k), waits=waits)
    g_done = mk.last("scalar")
    mk.op("sync", lambda e: e.nop(), waits=[("scalar", g_done)])


def emit_load_consts(mk):
    fs = mk.hbm("c_fsmall", [128, 8], F32)
    vm = mk.op("vector", lambda e: e.memset(mk.smallf[:, :], 0.0))
    mk.dma("sync", mk.smallf[:, 110:118], fs, waits=[("vector", vm)])
    jobs = [("c_ident16", [128, 128], mk.ident[:, :]),
            ("c_small16", [128, 960], mk.csmall[:, :]),
            ("c_masks16", [128, 1664], mk.masks[:, :]),
            ("c_perm16", [128, 128], mk.perm[:, :]),
            ("c_trim16", [128, 128], mk.trim[:, :])]
    for name, shape, dst in jobs:
        srcap = mk.hbm(name, shape, BF16)
        mk.dma("sync", dst, srcap)
    mk.hbm("c_tab16", [128, 4 * T], BF16)
    return mk.last("sync")


def build_program(debug_dump=(), blocks=None, fdepth=FDEPTH):
    mk = MK(debug_dump=debug_dump, blocks=blocks, fdepth=fdepth)
    mk.ws = WStream(mk)
    ld = emit_load_input(mk)
    cst = emit_load_consts(mk)
    mk.hbm("park", [D, T], F32, kind="Internal")
    mk.hbm("ot_scratch", [D, T], BF16, kind="Internal")
    if "attn_a" in mk.blocks:
        mk.hbm("aa_wqkvT", [D, 3 * D])
        mk.hbm("aa_woT", [D, D])
        mk.hbm("aa_wltm", [NH, DK, DK])
        mk.hbm("aa_wltmT", [NH, DK, DK])
        for nm in ("V_gs", "V_T0", "beta_tau", "beta_gm", "C_ch", "gamma"):
            mk.hbm("aa_" + nm, [NH])
        mk.hbm("aa_alpha_ppd", [1])
        mk.hbm("aa_IC_threshold", [1])
    if "attn_b" in mk.blocks:
        mk.hbm("ab_qT", [D, DK])
        mk.hbm("ab_kT", [D, DK])
        mk.hbm("ab_vT", [D, D])
        mk.hbm("ab_oT", [D, D])
        mk.hbm("ab_f1T", [D, 4 * D])
        mk.hbm("ab_f2T", [4 * D, D])
        for nm, sh in (("ln1_w", [D]), ("ln1_b", [D]), ("ln2_w", [D]),
                       ("ln2_b", [D]), ("q_b", [DK]), ("k_b", [DK]),
                       ("v_b", [D]), ("o_b", [D]), ("f1_b", [4 * D]),
                       ("f2_b", [D])):
            mk.hbm("ab_" + nm, sh)
    if "ffn1" in mk.blocks or "ffn2" in mk.blocks:
        mk.hbm("ff_Wqkv", [FDEPTH, D, 3 * D])
        mk.hbm("ff_q_scale", [FDEPTH, DK])
        mk.hbm("ff_k_scale", [FDEPTH, DK])
        mk.hbm("ff_Wo", [FDEPTH, D, D])
        mk.hbm("ff_ln_g", [FDEPTH, D])
        mk.hbm("ff_ln_b", [FDEPTH, D])
        mk.hbm("ff_W1", [FDEPTH, D, 2 * DI])
        mk.hbm("ff_b1", [FDEPTH, 2 * DI])
        mk.hbm("ff_W2", [FDEPTH, DI, D])
        mk.hbm("ff_b2", [FDEPTH, D])

    waits = [("scalar", ld), ("sync", cst)]
    if "lntest" in mk.blocks:
        cd = emit_cast_shift(mk, [mk.rk(k) for k in range(KT)], mk.sq,
                             shifted=("lnshift" in mk.blocks), waits=waits)
        vd = emit_ln(mk, mk.sq, mk.xb, cast_done=cd, extra_wait=tuple(waits))
        for k in range(KT):
            mk.dump(f"ln{k}", mk.bk(mk.xb, k), [("vector", vd)])
    from_blocks = []
    if "attn_a" in mk.blocks:
        waits = emit_attn_a(mk, waits)
    if "ffn1" in mk.blocks:
        waits = emit_ffn_module(mk, waits, call=0)
    if "attn_b" in mk.blocks:
        waits = emit_attn_b(mk, waits)
    if "ffn2" in mk.blocks:
        waits = emit_ffn_module(mk, waits, call=1)
    emit_store_output(mk, waits)
    return mk


def emit_attn_a(mk, waits):
    SEG = 1024
    onec = cs(mk, CS_ONECOL, 1)
    arF = mk.arena[:].bitcast(F32)
    ar2F = mk.ar2[:].bitcast(F32)
    sqF = mk.sq[:].bitcast(F32)

    VN = arF[0:16, 0:4096]
    SP = arF[0:16, 4096:8192]
    GCH = ar2F[0:16, 0:4096]
    T4 = sqF[0:16, 0:4096]
    COLS = {}
    names = ("V_gs", "V_T0", "beta_tau", "beta_gm", "C_ch", "gamma")
    for i, nm in enumerate(names):
        COLS[nm] = mk.smallf[0:16, 32 + i:33 + i]
        mk.dma("sync", COLS[nm], mk.dram["aa_" + nm].rearrange("(p o) -> p o", o=1),
               waits=list(waits) if i == 0 else (),
               allow_slow_non_contiguous=True)
    COLS["alpha"] = mk.smallf[0:16, 40:41]
    COLS["ic"] = mk.smallf[0:16, 41:42]
    mk.dma("sync", COLS["alpha"],
           mk.dram["aa_alpha_ppd"].rearrange("(o p) -> o p", o=1)
           .broadcast_to([16, 1]), allow_slow_non_contiguous=True)
    mk.dma("sync", COLS["ic"],
           mk.dram["aa_IC_threshold"].rearrange("(o p) -> o p", o=1)
           .broadcast_to([16, 1]), allow_slow_non_contiguous=True)
    mk.dma("sync", VN, mk.dram["aa_wltm"].rearrange("h d e -> h (d e)"))
    lcols = mk.last("sync")
    v0 = mk.op("vector", lambda e: e.tensor_scalar(
        out=VN, in0=VN, scalar1=COLS["V_gs"], scalar2=COLS["V_T0"],
        op0=ALU.add, op1=ALU.subtract), waits=[("sync", lcols)])
    mk.op("scalar", lambda e: e.activation(out=SP, in_=VN, func=AF.Abs),
          waits=[("vector", v0)])
    mk.op("scalar", lambda e: e.activation(out=SP, in_=SP, func=AF.Exp,
                                           scale=-1.0))
    mk.op("scalar", lambda e: e.activation(out=SP, in_=SP, func=AF.Ln,
                                           bias=1.0))
    s1 = mk.op("scalar", lambda e: e.activation(out=T4, in_=VN, func=AF.Relu))
    v1 = mk.op("vector", lambda e: e.tensor_add(out=SP, in0=SP, in1=T4),
               waits=[("scalar", s1)])
    mk.op("vector", lambda e: e.reciprocal(out=mk.smallf[0:16, 42:43],
                                           in_=COLS["C_ch"]))
    mk.op("vector", lambda e: e.tensor_scalar(
        out=GCH, in0=SP, scalar1=COLS["beta_tau"], scalar2=None,
        op0=ALU.mult, op1=ALU.bypass))
    v3 = mk.op("vector", lambda e: e.tensor_scalar(
        out=T4, in0=GCH, scalar1=mk.smallf[0:16, 42:43], scalar2=-1.0,
        op0=ALU.mult, op1=ALU.mult))
    s2 = mk.op("scalar", lambda e: e.activation(out=T4, in_=T4, func=AF.Exp),
               waits=[("vector", v3)])
    rc = mk.hbm("aa_rc", [2, 16, 4096], F32, kind="Internal")
    d1 = mk.dma("scalar", rc[0], T4, waits=[("scalar", s2)])
    v4 = mk.op("vector", lambda e: e.tensor_scalar(
        out=T4, in0=GCH, scalar1=COLS["ic"], scalar2=COLS["alpha"],
        op0=ALU.subtract, op1=ALU.mult), waits=[("scalar", d1)])
    s3 = mk.op("scalar", lambda e: e.activation(out=T4, in_=T4, func=AF.Tanh),
               waits=[("vector", v4)])
    s4 = mk.op("scalar", lambda e: e.activation(out=VN, in_=VN,
                                                func=AF.Sigmoid))
    v5 = mk.op("vector", lambda e: e.tensor_mul(out=SP, in0=SP, in1=VN),
               waits=[("scalar", s4)])
    mk.op("vector", lambda e: e.tensor_scalar(
        out=SP, in0=SP, scalar1=COLS["beta_gm"], scalar2=None,
        op0=ALU.mult, op1=ALU.bypass))
    v7 = mk.op("vector", lambda e: e.tensor_mul(out=SP, in0=SP, in1=T4),
               waits=[("scalar", s3)])
    v8 = mk.op("vector", lambda e: e.tensor_scalar(
        out=SP, in0=SP, scalar1=COLS["gamma"], scalar2=None,
        op0=ALU.mult, op1=ALU.bypass))
    d2 = mk.dma("scalar", rc[1], SP, waits=[("vector", v8)])
    pre_done = [("scalar", d2)]

    cd = emit_cast_shift(mk, [mk.rk(k) for k in range(KT)], mk.sq,
                         shifted=False, waits=list(waits) + pre_done)
    vd = emit_ln(mk, mk.sq, mk.xb, cast_done=cd, extra_wait=pre_done)
    x_wait = [("vector", vd)]

    VBSEL = mk.arena[:, 8192:8192 + 4096]
    MSUM = mk.arena[:, 13312:13312 + 2048]
    KBSELR = mk.arena[:, 15616:15616 + 128]
    WL16 = mk.arena[:, 15552:15552 + 64]
    mk.dma("sync", VBSEL, mk.hbm("c_vbsel16", [128, 4096], BF16),
           waits=list(x_wait))
    mk.dma("sync", MSUM, mk.hbm("c_msum16", [128, 2048], BF16))
    ld_sel = mk.dma("sync", KBSELR, mk.hbm("c_kbselr16", [128, 128], BF16))
    sel_wait = [("sync", ld_sel)]

    RCOL = arF[:, 7680:7712]
    CCOL = arF[:, 7712:7744]
    USTATE = arF[:, 7744:7776]

    QRAW = mk.ar2[:, 0:T]
    KN = mk.ar2[:, T:2 * T]
    VRAW = mk.ar2[:, 2 * T:3 * T]
    ZB = mk.ar2[:, 3 * T:3 * T + SEG]
    UB = mk.ar2[:, 3 * T + SEG:3 * T + 2 * SEG]
    Y2 = mk.ar2[:, 3 * T + 2 * SEG:3 * T + 2 * SEG + FCH]
    KBS = mk.ar2[:, 3 * T + 2 * SEG + FCH:3 * T + 2 * SEG + FCH + SEG]
    SC2 = mk.ar2[:, 11776:12288]
    wq = mk.dram["aa_wqkvT"]
    sel2 = cs(mk, CS_SEL2, 2)
    c2t = mk.csmall[0:2, CS_C2T:CS_C2T + 128]
    for hp in range(8):
        for idx, nt_off in enumerate((hp * 128, 1024 + hp * 128,
                                      2048 + hp * 128)):
            dstbuf = (QRAW, KN, VRAW)[idx]
            wap = wsrc3(wq[:, nt_off:nt_off + 128])
            wl = [("tensor", mk.last("tensor")), ("scalar", mk.last("scalar"))]
            ldw = mk.dma("sync", wdst3(mk.wst[:, 0:1024], KT), wap, waits=wl)
            cw = mk.op("scalar", lambda e: e.copy(
                out=mk.wbf[0][:, 0:1024], in_=mk.wst[:, 0:1024]),
                waits=[("sync", ldw)])
            for c in range(NCH):
                bank = mk.pb[4 + c % 2]
                for k in range(KT):
                    mk.op("tensor", lambda e, k=k, c=c, bank=bank:
                          e.matmul(out=bank[:, :],
                                   lhsT=mk.wbf[0][:, k * 128:(k + 1) * 128],
                                   rhs=mk.bk(mk.xb, k)[:, c * FCH:(c + 1) * FCH],
                                   start=(k == 0), stop=(k == KT - 1)),
                          waits=([("scalar", mk.last("scalar")),
                                  ("vector", mk.last("vector"))] + x_wait)
                          if k == 0 else (), inc=(1 if k == KT - 1 else 0))
                t_mm = mk.last("tensor")
                mk.op("scalar", lambda e, c=c, bank=bank, dstbuf=dstbuf:
                      e.activation(out=dstbuf[:, c * FCH:(c + 1) * FCH],
                                   in_=bank[:, :], func=AF.Copy),
                      waits=[("tensor", t_mm)])
        evac_qk = mk.last("scalar")

        MUK = mk.strow[0:2, 0:T]
        MSQK = ar2F[0:2, 4096:6144]
        MU2 = mk.wst[0:2, 0:T]
        ksq = mk.ar2[:, 3 * T:3 * T + 2 * SEG]
        v_sq = mk.op("vector", lambda e: e.tensor_mul(
            out=ksq, in0=KN, in1=KN), waits=[("scalar", evac_qk)])
        for c in range(NCH):
            wl = [("vector", v_sq)]
            if c > 0:
                wl.append(("scalar", mk.last("scalar")))
            mk.op("tensor", lambda e, c=c: e.matmul(
                out=mk.pb[6][0:2, :], lhsT=sel2,
                rhs=KN[:, c * FCH:(c + 1) * FCH], start=True, stop=True),
                waits=wl)
            t2 = mk.op("tensor", lambda e, c=c, ksq=ksq: e.matmul(
                out=mk.pb[7][0:2, :], lhsT=sel2,
                rhs=ksq[:, c * FCH:(c + 1) * FCH], start=True, stop=True),
                inc=1)
            mk.op("scalar", lambda e, c=c: e.activation(
                out=MUK[:, c * FCH:(c + 1) * FCH], in_=mk.pb[6][0:2, :],
                func=AF.Copy, scale=1.0 / DK), waits=[("tensor", t2)])
            mk.op("scalar", lambda e, c=c: e.activation(
                out=MSQK[:, c * FCH:(c + 1) * FCH], in_=mk.pb[7][0:2, :],
                func=AF.Copy, scale=1.0 / DK), inc=1)
        s_st = mk.last("scalar")
        v10a = mk.op("vector", lambda e: e.tensor_mul(out=MU2, in0=MUK, in1=MUK),
                     waits=[("scalar", s_st)])
        v10 = mk.op("vector", lambda e: e.tensor_sub(out=MSQK, in0=MSQK,
                                                     in1=MU2))
        mk.op("scalar", lambda e: e.activation(
            out=MSQK, in_=MSQK, func=AF.Ln, bias=mk.smallf[0:2, 110:111]),
            waits=[("vector", v10)])
        RSK = mk.srow16[0:2, 0:T]
        s6 = mk.op("scalar", lambda e: e.activation(
            out=RSK, in_=MSQK, func=AF.Exp, scale=-0.5))
        NMK = mk.srow16[0:2, T:2 * T]
        v11 = mk.op("vector", lambda e: e.scalar_tensor_tensor(
            out=NMK, in0=MUK, scalar=-1.0, in1=RSK, op0=ALU.mult,
            op1=ALU.mult), waits=[("scalar", s6)])
        for c in range(NCH):
            mk.op("tensor", lambda e, c=c: e.matmul(
                out=mk.pb[6][:, :], lhsT=c2t, rhs=RSK[0:2, c * FCH:(c + 1) * FCH],
                start=True, stop=True), waits=[("vector", v11),
                                               ("vector", mk.last("vector"))])
            t_bb = mk.op("tensor", lambda e, c=c: e.matmul(
                out=mk.pb[7][:, :], lhsT=c2t, rhs=NMK[0:2, c * FCH:(c + 1) * FCH],
                start=True, stop=True), inc=1)
            mk.op("vector", lambda e, c=c: e.tensor_tensor(
                out=SC2, in0=KN[:, c * FCH:(c + 1) * FCH], in1=mk.pb[6][:, :],
                op=ALU.mult), waits=[("tensor", t_bb)])
            mk.op("vector", lambda e, c=c: e.tensor_tensor(
                out=KN[:, c * FCH:(c + 1) * FCH], in0=SC2, in1=mk.pb[7][:, :],
                op=ALU.add))
        kn_done = mk.last("vector")

        for h2 in range(2):
            h = 2 * hp + h2
            hr = h2 * 64
            ldr = mk.dma("sync", RCOL,
                         mk.dram["aa_rc"][0, h].rearrange("(t p) -> p t", p=128),
                         waits=pre_done + [("vector", mk.last("vector"))],
                         allow_slow_non_contiguous=True)
            mk.dma("sync", CCOL,
                   mk.dram["aa_rc"][1, h].rearrange("(t p) -> p t", p=128),
                   allow_slow_non_contiguous=True)
            mk.dma("sync", mk.wst[0:64, 0:64], mk.dram["aa_wltmT"][h])
            ld_state = mk.last("sync")
            cwl = mk.op("vector", lambda e: e.tensor_copy(
                out=WL16[0:64, :], in_=mk.wst[0:64, 0:64]),
                waits=[("sync", ld_state)])
            if h2 == 1:
                ld_state = mk.dma("sync", WL16[64:128, :], WL16[0:64, :],
                                  waits=[("vector", cwl)])
                wlview = WL16[64:128, :]
            else:
                wlview = WL16[0:64, :]
            for seg in range(2):
                t0 = seg * SEG
                t_kb = mk.op("tensor", lambda e, hr=hr, t0=t0: e.matmul(
                    out=mk.pb[0][:, :],
                    lhsT=KBSELR[hr:hr + 64, :], rhs=KN[hr:hr + 64, t0:t0 + 512],
                    start=True, stop=True),
                    waits=[("vector", kn_done), ("sync", ld_state),
                           ("vector", cwl), ("vector", mk.last("vector")),
                           ("scalar", mk.last("scalar"))] + sel_wait)
                mk.op("tensor", lambda e, hr=hr, t0=t0: e.matmul(
                    out=mk.pb[1][:, :],
                    lhsT=KBSELR[hr:hr + 64, :],
                    rhs=KN[hr:hr + 64, t0 + 512:t0 + 1024],
                    start=True, stop=True))
                mk.op("tensor", lambda e, hr=hr, t0=t0: e.matmul(
                    out=mk.pb[2][:, :],
                    lhsT=KBSELR[hr:hr + 64, :], rhs=QRAW[hr:hr + 64, t0:t0 + 512],
                    start=True, stop=True))
                t_qb = mk.op("tensor", lambda e, hr=hr, t0=t0: e.matmul(
                    out=mk.pb[3][:, :],
                    lhsT=KBSELR[hr:hr + 64, :],
                    rhs=QRAW[hr:hr + 64, t0 + 512:t0 + 1024],
                    start=True, stop=True), inc=1)
                mk.op("scalar", lambda e: e.activation(
                    out=KBS[:, 0:512], in_=mk.pb[0][:, :], func=AF.Copy),
                    waits=[("tensor", t_qb)])
                s_kb = mk.op("scalar", lambda e: e.activation(
                    out=KBS[:, 512:1024], in_=mk.pb[1][:, :], func=AF.Copy))
                for c2 in range(2):
                    mk.op("tensor", lambda e, c2=c2, hr=hr, t0=t0,
                          wlview=wlview: e.matmul(
                        out=mk.pb[4 + c2][hr:hr + 64, :],
                        lhsT=wlview,
                        rhs=QRAW[hr:hr + 64, t0 + c2 * FCH:t0 + (c2 + 1) * FCH],
                        start=True, stop=False,
                        tile_position=(hr, hr)),
                        waits=[("scalar", s_kb)],
                        inc=(1 if c2 == 1 else 0))
                def emit_vbsel(t, wl):
                    for c2 in range(2):
                        mk.op("tensor", lambda e, t=t, c2=c2, hr=hr, t0=t0:
                              e.matmul(out=mk.pb[6 + c2][:, :],
                                       lhsT=VBSEL[hr:hr + 64,
                                                  t * 128:(t + 1) * 128],
                                       rhs=VRAW[hr:hr + 64,
                                                t0 + c2 * FCH:t0 + (c2 + 1) * FCH],
                                       start=True, stop=True),
                              waits=wl if c2 == 0 else (),
                              inc=(1 if c2 == 1 else 0))
                    return mk.last("tensor")

                pend_tvb = emit_vbsel(0, [("vector", mk.last("vector"))])
                for t in range(32):
                    t_vb = pend_tvb
                    for c2 in range(2):
                        mk.op("vector", lambda e, c2=c2:
                              e.tensor_tensor(
                                  out=ZB[:, c2 * 512:(c2 + 1) * 512],
                                  in0=mk.pb[6 + c2][:, :],
                                  in1=KBS[:, c2 * 512:(c2 + 1) * 512],
                                  op=ALU.mult),
                              waits=[("tensor", t_vb)] if c2 == 0 else ())
                    v_zb = mk.last("vector")
                    ini = 0.0 if seg == 0 else USTATE[:, t:t + 1]
                    mk.op("vector", lambda e, t=t, ini=ini:
                          e.tensor_tensor_scan(
                              out=UB, data0=RCOL[:, t:t + 1].broadcast_to([128, SEG]),
                              data1=ZB, initial=ini, op0=ALU.mult, op1=ALU.add))
                    if seg == 0:
                        mk.op("vector", lambda e, t=t: e.tensor_copy(
                            out=USTATE[:, t:t + 1], in_=UB[:, SEG - 1:SEG]))
                    if t < 31:
                        pend_tvb = emit_vbsel(t + 1, [("vector", v_zb)])
                    for c2 in range(2):
                        v_y2 = mk.op("vector", lambda e, c2=c2, t=t:
                                     e.scalar_tensor_tensor(
                                         out=Y2,
                                         in0=UB[:, c2 * 512:(c2 + 1) * 512],
                                         scalar=CCOL[:, t:t + 1],
                                         in1=mk.pb[2 + c2][:, :],
                                         op0=ALU.mult, op1=ALU.mult),
                                     waits=[("tensor", mk.last("tensor"))])
                        mk.op("tensor", lambda e, t=t, c2=c2, hr=hr:
                              e.matmul(out=mk.pb[4 + c2][hr:hr + 64, :],
                                       lhsT=MSUM[:, t * 64:(t + 1) * 64],
                                       rhs=Y2, start=False,
                                       stop=(t == 31),
                                       tile_position=(0, hr)),
                              waits=[("vector", v_y2)], inc=1)
                t_y = mk.last("tensor")
                for c2 in range(2):
                    mk.op("scalar", lambda e, c2=c2, hr=hr, hp=hp, t0=t0:
                          e.activation(
                              out=mk.sq[hr:hr + 64,
                                        hp * 2048 + t0 + c2 * FCH:
                                        hp * 2048 + t0 + (c2 + 1) * FCH],
                              in_=mk.pb[4 + c2][hr:hr + 64, :], func=AF.Copy),
                          waits=[("tensor", t_y)], inc=1)
    aa_done = mk.last("scalar")

    def wo_ap(nt):
        return wsrc3(mk.dram["aa_woT"][:, nt * 128:(nt + 1) * 128])

    emit_gemm_b(mk, wo_ap, 8, lambda k: mk.sq[:, k * 2048:(k + 1) * 2048],
                lambda nt, c, bank, t_mm: emit_resid_add(mk, nt, c, bank, t_mm),
                x_wait=[("scalar", aa_done), ("vector", mk.last("vector"))],
                cast_eng="scalar")
    return [("vector", mk.last("vector"))]


def emit_park(mk, waits):
    park = mk.dram["park"]
    for k in range(KT):
        mk.dma("scalar", park[k * 128:(k + 1) * 128, :], mk.rk(k), waits=waits)
    return mk.last("scalar")


def emit_unpark_add(mk, waits):
    park = mk.dram["park"]
    prev_add = None
    for k in range(KT):
        wl = list(waits)
        if prev_add is not None:
            wl.append(("vector", prev_add))
        ld = mk.dma("sync", mk.wst[:, :], park[k * 128:(k + 1) * 128, :],
                    waits=wl + [("scalar", mk.last("scalar"))])
        prev_add = mk.op("vector", lambda e, k=k:
                         e.tensor_tensor(out=mk.rk(k), in0=mk.rk(k),
                                         in1=mk.wst[:, :], op=ALU.add),
                         waits=[("sync", ld)])
    return mk.last("vector")


def ffn_attn_layer(mk, li, waits, call):
    wq = mk.dram["ff_Wqkv"]
    wo = mk.dram["ff_Wo"]
    ots = mk.dram["ot_scratch"]
    CSQ = mk.sq[:, 8192:8192 + T]
    SSQ = mk.sq[:, 8192 + T:8192 + 2 * T]
    CSK = mk.sq[:, 8192 + 2 * T:8192 + 3 * T]
    SSK = mk.sq[:, 8192 + 3 * T:8192 + 4 * T]
    QRAW = mk.ar2[:, 0:T]
    KRAW = mk.ar2[:, T:2 * T]
    SCR = mk.ar2[:, 2 * T:4 * T]
    QT = mk.ar2[:, 4 * T:5 * T]
    KTT = mk.ar2[:, 5 * T:6 * T]
    onec = cs(mk, CS_ONECOL, 1)
    ones_r = cs(mk, CS_ONEROW, FCH)[0:1, :]
    c2t = mk.csmall[0:2, CS_C2T:CS_C2T + 128]
    kbsel = cs(mk, CS_KBSEL, 128)[0:64, :]
    sel2 = cs(mk, CS_SEL2, 2)

    ld_tab = mk.dma("sync", mk.sq[:, 8192:16384], mk.dram["c_tab16"],
                    waits=waits)
    qsc = mk.smallf[:, 100:101]
    ksc = mk.smallf[:, 101:102]
    qscp = mk.smallf[:, 104:105]
    kscp = mk.smallf[:, 105:106]
    kbselp = cs(mk, CS_KBSELP, 128)[0:64, :]
    for i2, (name, dstcol, dstcolp) in enumerate(
            (("ff_q_scale", qsc, qscp), ("ff_k_scale", ksc, kscp))):
        stg = mk.smallf[:, 102 + i2:103 + i2]
        stg16 = mk.srow16[:, 4094 + i2:4095 + i2]
        srcap = mk.dram[name][li, :].rearrange("(p o) -> p o", o=1)
        ld = mk.dma("sync", stg[0:64, :], srcap, waits=list(waits),
                    allow_slow_non_contiguous=True)
        vc = mk.op("vector", lambda e, stg=stg, stg16=stg16:
                   e.tensor_copy(out=stg16[0:64, :], in_=stg[0:64, :]),
                   waits=[("sync", ld)])
        qksc = 8.0 if i2 == 0 else 1.0
        for sel, dc in ((kbsel, dstcol), (kbselp, dstcolp)):
            tm = mk.op("tensor", lambda e, stg16=stg16, sel=sel: e.matmul(
                out=mk.pb[0][:, 0:1], lhsT=sel, rhs=stg16[0:64, :],
                start=True, stop=True),
                waits=[("vector", vc), ("scalar", mk.last("scalar"))], inc=1)
            mk.op("scalar", lambda e, dc=dc, qksc=qksc: e.activation(
                out=dc, in_=mk.pb[0][:, 0:1], func=AF.Copy, scale=qksc),
                waits=[("tensor", tm)])
    setup_done = [("scalar", mk.last("scalar")), ("sync", ld_tab)]

    cd = emit_cast_shift(mk, [mk.rk(k) for k in range(KT)], mk.xb, shifted=True,
                         waits=list(waits))
    x_wait = [("vector", cd)]

    def v_w_ap(c):
        sl = wq[li, :, 2048 + c * 256:2048 + (c + 1) * 256]
        return wsrc3(sl)

    def v_out(c, tt, bank, t_mm):
        dst = mk.arena[:, tt * 1024 + c * 256:tt * 1024 + (c + 1) * 256]
        m = mk.op("scalar", lambda e, dst=dst, bank=bank:
                  e.activation(out=dst, in_=bank[:, 0:256], func=AF.Copy),
                  waits=[("tensor", t_mm)])
        return ("scalar", m)

    emit_gemm_a(mk, v_w_ap, 4, 256, lambda k: mk.bk(mk.xb, k), v_out,
                x_wait=x_wait + setup_done, cast_eng="scalar")
    seed = ("scalar", mk.last("scalar"))
    mk.aux_ev = {3: seed, 7: seed}

    for hp in range(8):
        for idx, nt_off in enumerate((hp * 128, 1024 + hp * 128)):
            dstbuf = QRAW if idx == 0 else KRAW
            wap = wsrc3(wq[li, :, nt_off:nt_off + 128])
            wbuf, cw = mk.ws.load([(wdst3(mk.wst[:, 0:1024], KT), wap)], 1024,
                                  extra_wait=list(waits))
            for c in range(NCH):
                bid = 3 if c % 2 == 0 else 7
                bank = mk.pb[bid]
                for k in range(KT):
                    wl2 = []
                    if k == 0:
                        wl2 = [("scalar", cw), mk.aux_ev[bid]] + x_wait
                    mk.op("tensor", lambda e, k=k, c=c, bank=bank, wbuf=wbuf:
                          e.matmul(out=bank[:, :],
                                   lhsT=wbuf[:, k * 128:(k + 1) * 128],
                                   rhs=mk.bk(mk.xb, k)[:, c * FCH:(c + 1) * FCH],
                                   start=(k == 0), stop=(k == KT - 1)),
                          waits=wl2, inc=(1 if k == KT - 1 else 0))
                t_mm = mk.last("tensor")
                sm = mk.op("scalar", lambda e, c=c, bank=bank, dstbuf=dstbuf:
                           e.activation(out=dstbuf[:, c * FCH:(c + 1) * FCH],
                                        in_=bank[:, :], func=AF.Copy),
                           waits=[("tensor", t_mm)])
                mk.aux_ev[bid] = ("scalar", sm)
            mk.ws.consumed(mk.last("tensor"))
        evac_qk = mk.last("scalar")

        for idx, (raw, dstT, CT, ST, scol, scolp) in enumerate(
                ((QRAW, QT, CSQ, SSQ, qsc, qscp),
                 (KRAW, KTT, CSK, SSK, ksc, kscp))):
            sqq = SCR[:, 0:T]
            v_sq = mk.op("vector", lambda e, raw=raw, sqq=sqq:
                         e.tensor_mul(out=sqq, in0=raw, in1=raw),
                         waits=[("scalar", evac_qk)])
            for c in range(NCH):
                bid = 3 if c % 2 == 0 else 7
                wl = [("vector", v_sq), mk.aux_ev[bid]]
                tm = mk.op("tensor", lambda e, c=c, sqq=sqq, bid=bid:
                           e.matmul(out=mk.pb[bid][0:2, :], lhsT=sel2,
                                    rhs=sqq[:, c * FCH:(c + 1) * FCH],
                                    start=True, stop=True),
                           waits=wl, inc=1)
                sm = mk.op("scalar", lambda e, c=c, bid=bid: e.activation(
                    out=mk.strow[0:2, c * FCH:(c + 1) * FCH],
                    in_=mk.pb[bid][0:2, :], func=AF.Copy),
                    waits=[("tensor", tm)])
                mk.aux_ev[bid] = ("scalar", sm)
            s_ss = mk.last("scalar")
            SS = mk.strow[0:2, 0:T]
            v_mx = mk.op("vector", lambda e: e.tensor_scalar_max(
                out=SS, in0=SS, scalar1=1e-24), waits=[("scalar", s_ss)])
            mk.op("scalar", lambda e: e.activation(out=SS, in_=SS, func=AF.Ln),
                  waits=[("vector", v_mx)])
            RN = mk.srow16[0:2, 0:T]
            s_rn = mk.op("scalar", lambda e: e.activation(
                out=RN, in_=SS, func=AF.Exp, scale=-0.5))
            RNB = SCR[:, T:2 * T]
            for c in range(NCH):
                bid = 3 if c % 2 == 0 else 7
                t_rnb = mk.op("tensor", lambda e, c=c, bid=bid: e.matmul(
                    out=mk.pb[bid][:, :], lhsT=c2t,
                    rhs=RN[0:2, c * FCH:(c + 1) * FCH], start=True, stop=True),
                    waits=[("scalar", s_rn), mk.aux_ev[bid]], inc=1)
                sm = mk.op("scalar", lambda e, c=c, RNB=RNB, bid=bid,
                           t_rnb=t_rnb: e.activation(
                    out=RNB[:, c * FCH:(c + 1) * FCH], in_=mk.pb[bid][:, :],
                    func=AF.Copy), waits=[("tensor", t_rnb)])
                mk.aux_ev[bid] = ("scalar", sm)
            s_rnb = mk.last("scalar")
            A = SCR[:, 0:T]
            v_A = mk.op("vector", lambda e, A=A, CT=CT, scol=scol, RNB=RNB:
                        e.scalar_tensor_tensor(out=A, in0=CT, scalar=scol,
                                               in1=RNB, op0=ALU.mult,
                                               op1=ALU.mult),
                        waits=[("scalar", s_rnb)])
            v_t1 = mk.op("vector", lambda e, raw=raw, dstT=dstT, A=A:
                         e.tensor_mul(out=dstT, in0=raw, in1=A))
            B = SCR[:, 0:T]
            v_B = mk.op("vector", lambda e, B=B, ST=ST, scolp=scolp, RNB=RNB:
                        e.scalar_tensor_tensor(out=B, in0=ST, scalar=scolp,
                                               in1=RNB, op0=ALU.mult,
                                               op1=ALU.mult))
            t2 = SCR[:, T:2 * T]
            for c in range(NCH):
                bid = 3 if c % 2 == 0 else 7
                t_pm = mk.op("tensor", lambda e, c=c, raw=raw, bid=bid:
                             e.matmul(
                    out=mk.pb[bid][:, :], lhsT=mk.perm[:, :],
                    rhs=raw[:, c * FCH:(c + 1) * FCH], start=True, stop=True),
                    waits=[("vector", v_B), mk.aux_ev[bid]], inc=1)
                vm = mk.op("vector", lambda e, c=c, t2=t2, B=B, bid=bid:
                           e.tensor_tensor(
                    out=t2[:, c * FCH:(c + 1) * FCH], in0=mk.pb[bid][:, :],
                    in1=B[:, c * FCH:(c + 1) * FCH], op=ALU.mult),
                    waits=[("tensor", t_pm)])
                mk.aux_ev[bid] = ("vector", vm)
            mk.op("vector", lambda e, dstT=dstT, t2=t2:
                  e.tensor_tensor(out=dstT, in0=dstT, in1=t2, op=ALU.add))
        prep_done = mk.last("vector")
        if hp == 0:
            mk.dump("qt_hp0", QT, [("vector", prep_done)])
            mk.dump("kt_hp0", KTT, [("vector", prep_done)])

        if not hasattr(mk, "attn_state"):
            mk.attn_state = {"exps": {}, "evac": {}, "wi": 0}
        ast = mk.attn_state
        wins = []
        for h2 in range(2):
            for w in range(NWIN):
                p = ast["wi"] % 2
                ast["wi"] += 1
                h = 2 * hp + h2
                hr = h2 * 64
                wins.append(dict(
                    h=h, hr=hr, w=w, p=p,
                    qh=QT[hr:hr + 64, :], kh=KTT[hr:hr + 64, :],
                    P1=mk.pb[0 if p == 0 else 4],
                    ST0=mk.pb[1 if p == 0 else 5],
                    ST1=mk.pb[2 if p == 0 else 6],
                    ST01=mk.pbig[:, FCH:3 * FCH] if p == 0
                    else mk.pbig[:, 5 * FCH:7 * FCH],
                    PT=SCR[:, p * 1024:(p + 1) * 1024],
                    PSC=SCR[:, 2048 + p * 512:2048 + (p + 1) * 512],
                    NMC=SCR[:, 3776 + p * 32:3776 + p * 32 + 4],
                    DCOL=SCR[:, 3776 + p * 32 + 4:3776 + p * 32 + 8],
                    LNC=SCR[:, 3776 + p * 32 + 8:3776 + p * 32 + 12],
                    NMR=SCR[:, 3200 + p * 256:3200 + (p + 1) * 256],
                    j0=max(0, (w - 1) * WIN),
                    jlen=WIN if w == 0 else 2 * WIN,
                    i0=w * WIN, njt=2 if w == 0 else 4))

        def front_it(c, it):
            PB = c["P1"] if it == 0 else c["ST1"]
            NMC, PSC = c["NMC"], c["PSC"]
            dcf = c["DCOL"].bitcast(F32)
            jlen, j0, i0, w, p = c["jlen"], c["j0"], c["i0"], c["w"], c["p"]
            wl = [("vector", prep_done)]
            key = ("p1", p) if it == 0 else ("st", p)
            if key in ast["exps"]:
                wl.append(("scalar", ast["exps"][key]))
            mk.op("tensor", lambda e, it=it, c=c, jlen=jlen, j0=j0, i0=i0,
                  PB=PB: e.matmul(out=PB[:, 0:jlen],
                                  lhsT=c["qh"][:, i0 + it * 128:
                                               i0 + it * 128 + 128],
                                  rhs=c["kh"][:, j0:j0 + jlen],
                                  start=True, stop=False),
                  waits=wl)
            msk = m1(mk, it) if w > 0 else m1w0(mk, it)
            t_s1 = mk.op("tensor", lambda e, msk=msk, jlen=jlen, PB=PB:
                         e.matmul(out=PB[:, 0:jlen], lhsT=mk.ident[:, :],
                                  rhs=msk[:, 0:jlen],
                                  start=False, stop=True), inc=1)
            v_m = mk.op("vector", lambda e, it=it, jlen=jlen, PB=PB, NMC=NMC:
                        e.reduce_max(out=NMC[:, it:it + 1], in_=PB[:, 0:jlen],
                                     axis=AX.X, negate=True),
                        waits=[("tensor", t_s1)])
            mk.op("scalar", lambda e, it=it, jlen=jlen, PB=PB, NMC=NMC,
                  PSC=PSC, dcf=dcf:
                  e.activation(out=PSC[:, 0:jlen], in_=PB[:, 0:jlen],
                               func=AF.Exp, bias=NMC[:, it:it + 1],
                               accum_out=dcf[:, it:it + 1]),
                  waits=[("vector", v_m)])

        def front_b0(c):
            LNC, NMC = c["LNC"], c["NMC"]
            dcf = c["DCOL"].bitcast(F32)
            s_l = mk.op("scalar", lambda e, dcf=dcf: e.activation(
                out=dcf[:, 0:2], in_=dcf[:, 0:2], func=AF.Ln))
            c["v_b"] = mk.op("vector", lambda e, LNC=LNC, dcf=dcf, NMC=NMC:
                             e.scalar_tensor_tensor(
                                 out=LNC[:, 0:2], in0=dcf[:, 0:2], scalar=-1.0,
                                 in1=NMC[:, 0:2], op0=ALU.mult, op1=ALU.add),
                             waits=[("scalar", s_l)])

        def front_b1(c):
            P1, LNC, NMR = c["P1"], c["LNC"], c["NMR"]
            for it in range(2):
                mk.op("tensor", lambda e, it=it, P1=P1, LNC=LNC: e.matmul(
                    out=P1[0:1, it * 128:(it + 1) * 128],
                    lhsT=LNC[:, it:it + 1], rhs=mk.ident[:, :],
                    start=(it == 0), stop=(it == 1)),
                    waits=[("vector", c["v_b"])], inc=(1 if it == 1 else 0))
            t_nr = mk.last("tensor")
            s_nr = mk.op("scalar", lambda e, P1=P1, NMR=NMR: e.activation(
                out=NMR[0:1, 0:256], in_=P1[0:1, 0:256], func=AF.Copy),
                waits=[("tensor", t_nr)])
            ast["exps"][("p1", c["p"])] = s_nr
            c["s_nr"] = s_nr

        def back_st(c):
            ST0, ST1, NMR, PT = c["ST0"], c["ST1"], c["NMR"], c["PT"]
            njt, j0, i0, w, p = c["njt"], c["j0"], c["i0"], c["w"], c["p"]
            s_nr = c["s_nr"]
            for jt in range(njt):
                bank = ST0 if jt < 2 else ST1
                boff = (jt % 2) * 256
                mjt = jt + 2 if w == 0 else jt
                wl = [("scalar", s_nr)]
                if jt % 2 == 0 and ("st", p) in ast["exps"]:
                    wl.append(("scalar", ast["exps"][("st", p)]))
                if jt == 0 and ("st0", p) in ast["evac"]:
                    wl.append(("scalar", ast["evac"][("st0", p)]))
                mk.op("tensor", lambda e, bank=bank, boff=boff, NMR=NMR:
                      e.matmul(out=bank[:, boff:boff + 256],
                               lhsT=ones_r[:, 0:128], rhs=NMR[0:1, 0:256],
                               start=(boff == 0), stop=False),
                      waits=wl)
                mk.op("tensor", lambda e, bank=bank, boff=boff, jt=jt, c=c,
                      j0=j0, i0=i0:
                      e.matmul(out=bank[:, boff:boff + 256],
                               lhsT=c["kh"][:, j0 + jt * 128:
                                            j0 + jt * 128 + 128],
                               rhs=c["qh"][:, i0:i0 + 256],
                               start=False, stop=False))
                last_of_pair = (jt % 2 == 1) or (w == 0 and jt == njt - 1)
                mk.op("tensor", lambda e, bank=bank, boff=boff, mjt=mjt,
                      lp=last_of_pair:
                      e.matmul(out=bank[:, boff:boff + 256],
                               lhsT=mk.ident[:, :], rhs=m2(mk, mjt),
                               start=False, stop=lp),
                      inc=(1 if last_of_pair else 0))
                if last_of_pair:
                    c["t_st%d" % (jt // 2)] = mk.last("tensor")
            ew = 512 * ((njt + 1) // 2)
            wl = [("tensor", c["t_st0"])]
            if njt > 2:
                wl.append(("tensor", c["t_st1"]))
            s_exp = mk.op("scalar", lambda e, c=c, ew=ew, PT=PT:
                          e.activation(out=PT[:, 0:ew],
                                       in_=c["ST01"][:, 0:ew],
                                       func=AF.Exp),
                          waits=wl)
            ast["exps"][("st", p)] = s_exp
            c["s_exp"] = s_exp

        def back_av(c):
            ST0, PT = c["ST0"], c["PT"]
            njt, j0, hr, h, w, p = (c["njt"], c["j0"], c["hr"], c["h"],
                                    c["w"], c["p"])
            for jt in range(njt):
                jtile = (j0 // 128) + jt
                mk.op("tensor", lambda e, jtile=jtile, jt=jt, h=h, hr=hr,
                      njt=njt, ST0=ST0, PT=PT:
                      e.matmul(out=ST0[hr:hr + 64, 0:256],
                               lhsT=mk.arena[:, jtile * 1024 + h * 64:
                                             jtile * 1024 + h * 64 + 64],
                               rhs=PT[:, jt * 256:(jt + 1) * 256],
                               start=(jt == 0), stop=(jt == njt - 1),
                               tile_position=(0, hr)),
                      waits=[("scalar", c["s_exp"])] if jt == 0 else (),
                      inc=(1 if jt == njt - 1 else 0))
            t_av = mk.last("tensor")
            ev = mk.op("scalar", lambda e, hr=hr, w=w, hp=hp, ST0=ST0:
                       e.activation(out=mk.sq[hr:hr + 64,
                                              (hp % 2) * 2048 + w * WIN:
                                              (hp % 2) * 2048 + (w + 1) * WIN],
                                    in_=ST0[hr:hr + 64, 0:256],
                                    func=AF.Copy),
                       waits=[("tensor", t_av)])
            ast["evac"][("st0", p)] = ev

        for i, c in enumerate(wins):
            if i == 0:
                front_it(c, 0)
                front_it(c, 1)
                front_b0(c)
            nxt = wins[i + 1] if i + 1 < len(wins) else None
            if nxt is not None:
                front_it(nxt, 0)
                front_it(nxt, 1)
            if i > 0:
                back_av(wins[i - 1])
            front_b1(c)
            back_st(c)
            if nxt is not None:
                front_b0(nxt)
        back_av(wins[-1])
        s_ot = mk.last("scalar")
        mk.dma("scalar", ots[hp * 128:(hp + 1) * 128, :],
               mk.sq[:, (hp % 2) * 2048:(hp % 2) * 2048 + 2048],
               waits=[("scalar", s_ot)])
    attn_done = mk.last("scalar")

    for k in range(KT):
        mk.dma("sync", mk.sq[:, k * 2048:(k + 1) * 2048],
               ots[k * 128:(k + 1) * 128, :],
               waits=[("scalar", attn_done)] if k == 0 else ())
    ot_ld = mk.last("sync")

    def wo_ap(nt):
        sl = wo[li, :, nt * 128:(nt + 1) * 128]
        return wsrc3(sl)

    emit_gemm_b(mk, wo_ap, 8, lambda k: mk.sq[:, k * 2048:(k + 1) * 2048],
                lambda nt, c, bank, t_mm: emit_resid_add(mk, nt, c, bank, t_mm),
                x_wait=[("sync", ot_ld)], cast_eng="scalar")
    return [("vector", mk.last("vector"))]


def ffn_ffn_layer(mk, li, waits, call):
    w1 = mk.dram["ff_W1"]
    w2 = mk.dram["ff_W2"]
    G16 = mk.arena[0:1, 12800:12800 + D]
    B16 = mk.arena[0:1, 12800 + D:12800 + 2 * D]
    ldg = mk.dma("sync", mk.strow[0:1, 0:D], mk.dram["ff_ln_g"][li:li + 1, :],
                 waits=list(waits) + [("vector", mk.last("vector"))])
    vg = mk.op("vector", lambda e: e.tensor_copy(out=G16,
                                                 in_=mk.strow[0:1, 0:D]),
               waits=[("sync", ldg)])
    ldb = mk.dma("sync", mk.strow[0:1, D:2 * D],
                 mk.dram["ff_ln_b"][li:li + 1, :], waits=[("vector", vg)])
    aff_done = mk.op("vector", lambda e: e.tensor_copy(
        out=B16, in_=mk.strow[0:1, D:2 * D]), waits=[("sync", ldb)])
    b1 = mk.dram["ff_b1"][li, :]
    b2 = mk.dram["ff_b2"][li, :]
    mk.dma("sync", mk.smallf[:, 0:21],
           b1[0:2688].rearrange("(f p) -> p f", p=128),
           waits=[("vector", aff_done)], allow_slow_non_contiguous=True)
    mk.dma("sync", mk.smallf[0:42, 21:22],
           b1[2688:2730].rearrange("(p o) -> p o", o=1))
    mk.dma("sync", mk.smallf[:, 22:43],
           b1[2730:5418].rearrange("(f p) -> p f", p=128),
           allow_slow_non_contiguous=True)
    mk.dma("sync", mk.smallf[0:42, 43:44],
           b1[5418:5460].rearrange("(p o) -> p o", o=1))
    mk.dma("sync", mk.smallf[:, 44:52], b2.rearrange("(f p) -> p f", p=128),
           allow_slow_non_contiguous=True)
    bias_done = mk.last("sync")

    cd = emit_cast_shift(mk, [mk.rk(k) for k in range(KT)], mk.sq, shifted=True,
                         waits=list(waits))
    vd = emit_ln(mk, mk.sq, mk.xb, g_row=G16, b_row=B16, cast_done=cd,
                 extra_wait=[("vector", aff_done)])
    x_wait = [("vector", vd), ("sync", bias_done)]

    NA = (DI + 127) // 128
    w2ev = {}
    for tc in range(2):
        tok0 = tc * 1024
        for pi in range(NA):
            aoff = pi * 128
            awid = min(128, DI - aoff)
            goff = DI + aoff
            jobs = []
            for wi, noff in enumerate((aoff, goff)):
                wap = wsrc3(w1[li, :, noff:noff + awid])
                jobs.append((wdst3(mk.wst[:, wi * 1024:wi * 1024 + KT * awid],
                                   KT), wap))
            wbuf, cw = mk.ws.load(jobs, 2048, extra_wait=list(waits))
            for wi in range(2):
                for c2 in range(2):
                    bank = mk.pb[wi * 2 + c2]
                    for k in range(KT):
                        wl2 = []
                        if k == 0:
                            wl2 = [("scalar", cw)] + x_wait + \
                                [("vector", mk.last("vector"))]
                        mk.op("tensor", lambda e, wi=wi, c2=c2, k=k, bank=bank,
                              awid=awid, tok0=tok0, wbuf=wbuf:
                              e.matmul(out=bank[0:awid, 0:FCH],
                                       lhsT=wbuf[:, wi * 1024 + k * awid:
                                                 wi * 1024 + k * awid + awid],
                                       rhs=mk.bk(mk.xb, k)[:, tok0 + c2 * FCH:
                                                           tok0 + (c2 + 1) * FCH],
                                       start=(k == 0), stop=(k == KT - 1)),
                              waits=wl2, inc=(1 if k == KT - 1 else 0))
            t_w1 = mk.last("tensor")
            mk.ws.consumed(t_w1)
            b1a = mk.smallf[:, pi:pi + 1]
            b1g = mk.smallf[:, 22 + pi:23 + pi]
            GEL = mk.ar2[:, 10240:11264]
            for c2 in range(2):
                mk.op("scalar", lambda e, c2=c2, b1g=b1g:
                      e.activation(out=GEL[:, c2 * FCH:(c2 + 1) * FCH],
                                   in_=mk.pb[2 + c2][:, :], func=AF.Gelu,
                                   bias=b1g),
                      waits=[("tensor", t_w1)])
            s_gel = mk.last("scalar")
            hhv = hh_view(mk, pi)
            for c2 in range(2):
                mk.op("vector", lambda e, c2=c2, hhv=hhv, b1a=b1a:
                      e.scalar_tensor_tensor(
                          out=hhv[:, c2 * FCH:(c2 + 1) * FCH],
                          in0=mk.pb[c2][:, :], scalar=b1a,
                          in1=GEL[:, c2 * FCH:(c2 + 1) * FCH],
                          op0=ALU.add, op1=ALU.mult),
                      waits=[("scalar", s_gel)])
        v_hh = mk.last("vector")
        for nt in range(KT):
            wbuf1, cw1 = mk.ws.load(
                [(wdst3(mk.wst[:, 0:2048], 16),
                  wsrc3(w2[li, 0:2048, nt * 128:(nt + 1) * 128]))], 2048)
            for c2 in range(2):
                bank = mk.pb[4 + (nt % 2) * 2 + c2]
                for k in range(16):
                    wl2 = []
                    if k == 0:
                        wl2 = [("scalar", cw1), ("vector", v_hh)]
                        bkey = (4 + (nt % 2) * 2 + c2)
                        if bkey in w2ev:
                            wl2.append(("vector", w2ev[bkey]))
                    mk.op("tensor", lambda e, k=k, c2=c2, bank=bank, tc=tc,
                          wbuf1=wbuf1:
                          e.matmul(out=bank[:, :],
                                   lhsT=wbuf1[:, k * 128:(k + 1) * 128],
                                   rhs=hh_view(mk, k)[:, c2 * FCH:
                                                      (c2 + 1) * FCH],
                                   start=(k == 0), stop=False),
                          waits=wl2,
                          inc=(1 if (c2 == 1 and k == 15) else 0))
            t_p1 = mk.last("tensor")
            mk.ws.consumed(t_p1)
            wbuf2, cw2 = mk.ws.load(
                [(wdst3(mk.wst[:, 0:640], 5),
                  wsrc3(w2[li, 2048:2688, nt * 128:(nt + 1) * 128])),
                 (mk.wst[0:42, 640:768],
                  w2[li, 2688:2730, nt * 128:(nt + 1) * 128])], 768)
            for c2 in range(2):
                bank = mk.pb[4 + (nt % 2) * 2 + c2]
                for k in range(6):
                    kk = 16 + k
                    rows = 128 if k < 5 else 42
                    mk.op("tensor", lambda e, k=k, c2=c2, bank=bank, rows=rows,
                          kk=kk, wbuf2=wbuf2:
                          e.matmul(out=bank[:, :],
                                   lhsT=wbuf2[0:rows, k * 128:k * 128 + 128],
                                   rhs=hh_view(mk, kk)[0:rows,
                                                       c2 * FCH:(c2 + 1) * FCH],
                                   start=False, stop=(k == 5)),
                          waits=[("scalar", cw2)] if k == 0 else (),
                          inc=(1 if k == 5 else 0))
                t_w2 = mk.last("tensor")
                b2c = mk.smallf[:, 44 + nt:45 + nt]
                dst = mk.rk(nt)[:, tok0 + c2 * FCH:tok0 + (c2 + 1) * FCH]
                w2ev[4 + (nt % 2) * 2 + c2] = mk.op(
                    "vector", lambda e, dst=dst, bank=bank, b2c=b2c:
                    e.scalar_tensor_tensor(out=dst, in0=bank[:, :],
                                           scalar=b2c, op0=ALU.add,
                                           in1=dst, op1=ALU.add),
                    waits=[("tensor", t_w2)])
            mk.ws.consumed(mk.last("tensor"))
    return [("vector", mk.last("vector"))]


def hh_view(mk, i):
    if i < 16:
        return mk.arena[:, i * 1024:(i + 1) * 1024]
    return mk.ar2[:, 4096 + (i - 16) * 1024:4096 + (i - 15) * 1024]


def emit_ffn_module(mk, waits, call):
    cd = emit_cast_shift(mk, [mk.rk(k) for k in range(KT)], mk.sq,
                         shifted=False, waits=waits)
    pk = emit_park(mk, list(waits) + [("vector", cd)])
    vd = emit_ln(mk, mk.sq, None, cast_done=cd,
                 extra_wait=[("scalar", pk)],
                 out_tiles_f32=[mk.rk(k) for k in range(KT)])
    w = [("vector", vd)]
    for li in range(mk.fdepth):
        for k in range(KT):
            mk.dump(f"c{call}l{li}_pre{k}", mk.rk(k), w)
        w = ffn_attn_layer(mk, li, w, call)
        for k in range(KT):
            mk.dump(f"c{call}l{li}_pa{k}", mk.rk(k), w)
        w = ffn_ffn_layer(mk, li, w, call)
    vu = emit_unpark_add(mk, w)
    return [("vector", vu)]


def emit_attn_b(mk, waits):
    CCH = 128
    NCC = T // CCH
    ots = mk.dram["ot_scratch"]
    cd = emit_cast_shift(mk, [mk.rk(k) for k in range(KT)], mk.sq,
                         shifted=False, waits=waits)
    pk = emit_park(mk, list(waits) + [("vector", cd)])
    vd = emit_ln(mk, mk.sq, None, cast_done=cd, extra_wait=[("scalar", pk)],
                 out_tiles_f32=[mk.rk(k) for k in range(KT)])
    w = [("vector", vd)]

    LN1G = mk.arena[0:1, 12800:12800 + D]
    LN1B = mk.arena[0:1, 12800 + D:12800 + 2 * D]
    ms = None
    for name, dst in (("ab_ln1_w", LN1G), ("ab_ln1_b", LN1B)):
        ld = mk.dma("sync", mk.strow[0:1, 0:D],
                    mk.dram[name].rearrange("(o f) -> o f", o=1),
                    waits=list(w) if ms is None else [("vector", ms)])
        ms = mk.op("vector", lambda e, dst=dst: e.tensor_copy(
            out=dst, in_=mk.strow[0:1, 0:D]), waits=[("sync", ld)])
    QBC = mk.smallf[:, 60:61]
    KBC = mk.smallf[:, 61:62]
    OBC = mk.smallf[:, 64:72]
    F2BC = mk.smallf[:, 72:80]
    F1BC = mk.smallf[:, 0:32]
    ld = mk.dma("sync", QBC[0:64, :], mk.dram["ab_q_b"].rearrange("(p o) -> p o", o=1),
                waits=[("vector", ms)], allow_slow_non_contiguous=True)
    ld = mk.dma("sync", KBC[0:64, :], mk.dram["ab_k_b"].rearrange("(p o) -> p o", o=1),
                allow_slow_non_contiguous=True)
    ld = mk.dma("sync", OBC, mk.dram["ab_o_b"].rearrange("(f p) -> p f", p=128),
                allow_slow_non_contiguous=True)
    ld = mk.dma("sync", F2BC, mk.dram["ab_f2_b"].rearrange("(f p) -> p f", p=128),
                allow_slow_non_contiguous=True)
    ld = mk.dma("sync", F1BC, mk.dram["ab_f1_b"].rearrange("(f p) -> p f", p=128),
                allow_slow_non_contiguous=True)
    bias_done = [("vector", ms), ("sync", mk.last("sync"))]

    cd1 = emit_cast_shift(mk, [mk.rk(k) for k in range(KT)], mk.sq,
                          shifted=False, waits=list(w))
    v1 = emit_ln(mk, mk.sq, mk.xb, g_row=LN1G, b_row=LN1B, cast_done=cd1,
                 extra_wait=bias_done)
    x_wait = [("vector", v1)]
    VBR = mk.srow16[0:1, 0:D]
    ld = mk.dma("sync", mk.strow[0:1, D:2 * D],
                mk.dram["ab_v_b"].rearrange("(o f) -> o f", o=1),
                waits=list(x_wait) + [("vector", mk.last("vector"))])
    vbm = mk.op("vector", lambda e: e.tensor_copy(out=VBR,
                                                  in_=mk.strow[0:1, D:2 * D]),
                waits=[("sync", ld)])
    x_wait = x_wait + [("vector", vbm)]

    QF = mk.ar2[:, 0:T]
    KF = mk.ar2[:, T:2 * T]
    ES = mk.ar2[:, 2 * T:3 * T].bitcast(F32)
    for qi, (wname, dstb, bcol) in enumerate(
            (("ab_qT", QF, QBC), ("ab_kT", KF, KBC))):
        wap = wsrc3(mk.dram[wname][:, 0:64])
        wbuf, cw = mk.ws.load([(wdst3(mk.wst[:, 0:KT * 64], KT), wap)],
                              KT * 64, extra_wait=list(w))
        for c in range(NCH):
            bank = mk.pb[c % 2]
            for k in range(KT):
                mk.op("tensor", lambda e, k=k, c=c, bank=bank, wbuf=wbuf:
                      e.matmul(out=bank[0:64, :],
                               lhsT=wbuf[:, k * 64:(k + 1) * 64],
                               rhs=mk.bk(mk.xb, k)[:, c * FCH:(c + 1) * FCH],
                               start=(k == 0), stop=(k == KT - 1)),
                      waits=([("scalar", mk.last("scalar")), ("scalar", cw)]
                             + x_wait)
                      if k == 0 else (), inc=(1 if k == KT - 1 else 0))
            t_mm = mk.last("tensor")
            X1 = ES[0:64, 0:FCH]
            X2 = ES[0:64, FCH:2 * FCH]
            vm1 = mk.op("vector", lambda e, bank=bank, bcol=bcol:
                        e.tensor_scalar(out=X1, in0=bank[0:64, :],
                                        scalar1=bcol[0:64, :], scalar2=0.0,
                                        op0=ALU.add, op1=ALU.min),
                        waits=[("tensor", t_mm)])
            se = mk.op("scalar", lambda e: e.activation(out=X1, in_=X1,
                                                        func=AF.Exp),
                       waits=[("vector", vm1)])
            mk.op("vector", lambda e, bank=bank, bcol=bcol:
                  e.tensor_scalar(out=X2, in0=bank[0:64, :], scalar1=bcol[0:64, :],
                                  scalar2=0.0, op0=ALU.add, op1=ALU.max))
            mk.op("vector", lambda e, c=c, dstb=dstb:
                  e.tensor_add(out=dstb[0:64, c * FCH:(c + 1) * FCH],
                               in0=X1, in1=X2), waits=[("scalar", se)])
        mk.ws.consumed(mk.last("tensor"))
    qk_done = mk.last("vector")
    mk.dump("ab_qf", QF[0:64, :], [("vector", qk_done)])
    mk.dump("ab_kf", KF[0:64, :], [("vector", qk_done)])

    onec = cs(mk, CS_ONECOL, 1)
    ones_r = cs(mk, CS_ONEROW, FCH)[0:1, :]

    def v_w_ap(c):
        return wsrc3(mk.dram["ab_vT"][:, c * 256:(c + 1) * 256])

    def v_out(c, tt, bank, t_mm):
        dst = mk.arena[:, tt * 1024 + c * 256:tt * 1024 + (c + 1) * 256]
        m = mk.op("scalar", lambda e, dst=dst, bank=bank:
                  e.activation(out=dst, in_=bank[:, 0:256], func=AF.Copy),
                  waits=[("tensor", t_mm)])
        return ("scalar", m)

    for c in range(4):
        wbuf, cw = mk.ws.load([(wdst3(mk.wst[:, 0:KT * 256], KT), v_w_ap(c))],
                              KT * 256, extra_wait=list(w))
        for tt in range(16):
            bank = mk.pb[2 + (tt % 2)]
            for k in range(KT):
                mk.op("tensor", lambda e, k=k, tt=tt, bank=bank, c=c, wbuf=wbuf:
                      e.matmul(out=bank[:, 0:256],
                               lhsT=mk.bk(mk.xb, k)[:, tt * 128:(tt + 1) * 128],
                               rhs=wbuf[:, k * 256:(k + 1) * 256],
                               start=(k == 0), stop=False),
                      waits=([("scalar", mk.last("scalar")), ("scalar", cw)]
                             + x_wait + [("vector", qk_done)])
                      if k == 0 else ())
            mk.op("tensor", lambda e, bank=bank, c=c:
                  e.matmul(out=bank[:, 0:256], lhsT=ones_r[:, 0:128],
                           rhs=VBR[:, c * 256:(c + 1) * 256],
                           start=False, stop=True), inc=1)
            t_mm = mk.last("tensor")
            if tt == 15:
                mk.ws.consumed(t_mm)
            if c == 0 and tt == 0:
                scr = mk.ar2[:, 3 * T + 512:3 * T + 1024].bitcast(F32)
                sm = mk.op("scalar", lambda e, bank=bank, scr=scr: e.activation(
                    out=scr, in_=bank[:, 0:256], func=AF.Copy),
                    waits=[("tensor", t_mm)])
                mk.dump("ab_pbv", scr, [("scalar", sm)])
            v_out(c, tt, bank, t_mm)
    v_done = mk.last("scalar")
    mk.dump("ab_vtok0", mk.arena[:, 0:1024], [("scalar", v_done)])

    KTOK = mk.ar2[:, 2 * T:2 * T + 1024]
    pbb = mk.pb[4][:, :].bitcast(BF16)
    for tt in range(16):
        tr = mk.op("tensor", lambda e, tt=tt: e.transpose(
            out=pbb[0:128, 0:64], in_=KF[0:64, tt * 128:(tt + 1) * 128],
            identity=mk.ident[0:64, 0:64]),
            waits=[("vector", qk_done)] if tt == 0 else
            [("scalar", mk.last("scalar"))], inc=1)
        mk.op("scalar", lambda e, tt=tt: e.activation(
            out=KTOK[:, tt * 64:(tt + 1) * 64], in_=pbb[0:128, 0:64],
            func=AF.Copy), waits=[("tensor", tr)])
    ktok_done = mk.last("scalar")

    KV16 = mk.ar2[:, 2 * T + 1024:2 * T + 2048]
    KS16 = mk.ar2[0:64, 3 * T:3 * T + 1]
    ATM = mk.ar2[:, 3 * T + 256:3 * T + 384]
    KVF = mk.ar2[:, 6656:8704].bitcast(F32)
    KSF = mk.ar2[:, 8704:8708].bitcast(F32)
    YST = mk.sq
    DENF = mk.strow[0:1, 0:CCH]
    RROW = mk.srow16[0:1, 0:CCH]
    trim = mk.trim
    for cc in range(NCC):
        i0 = cc * CCH
        if cc > 0:
            mk.op("vector", lambda e: e.tensor_copy(out=KV16[0:64, :],
                                                    in_=KVF[0:64, :]))
            mk.op("vector", lambda e: e.tensor_copy(out=KS16, in_=KSF[0:64, 0:1]),
                  inc=1)
        v_state = mk.last("vector")
        t_at = mk.op("tensor", lambda e, i0=i0: e.matmul(
            out=mk.pb[0][:, 0:CCH], lhsT=KF[0:64, i0:i0 + CCH],
            rhs=QF[0:64, i0:i0 + CCH], start=True, stop=True),
            waits=[("vector", v_state), ("scalar", ktok_done),
                   ("scalar", mk.last("scalar"))], inc=1)
        v_atm = mk.op("vector", lambda e: e.tensor_tensor(
            out=ATM, in0=mk.pb[0][:, 0:CCH], in1=trim[:, :], op=ALU.mult),
            waits=[("tensor", t_at)])
        if cc > 0:
            mk.op("tensor", lambda e, i0=i0: e.matmul(
                out=mk.pb[4][0:1, 0:CCH], lhsT=KS16[:, :],
                rhs=QF[0:64, i0:i0 + CCH], start=True, stop=False),
                waits=[("vector", v_atm)])
        t_den = mk.op("tensor", lambda e, cc=cc: e.matmul(
            out=mk.pb[4][0:1, 0:CCH], lhsT=onec[0:128, :], rhs=ATM,
            start=(cc == 0), stop=True),
            waits=[("vector", v_atm)] if cc == 0 else (), inc=1)
        mk.op("vector", lambda e: e.tensor_scalar_add(
            out=DENF, in0=mk.pb[4][0:1, 0:CCH], scalar1=1e-6),
            waits=[("tensor", t_den)])
        mk.op("vector", lambda e: e.reciprocal(out=DENF, in_=DENF))
        v_rec = mk.op("vector", lambda e: e.tensor_copy(out=RROW, in_=DENF))
        t_rb = mk.op("tensor", lambda e: e.matmul(
            out=mk.pb[4][:, 0:CCH], lhsT=ones_r[:, 0:128], rhs=RROW,
            start=True, stop=True), waits=[("vector", v_rec)], inc=1)
        v_a2 = mk.op("vector", lambda e: e.tensor_tensor(
            out=ATM, in0=ATM, in1=mk.pb[4][:, 0:CCH], op=ALU.mult),
            waits=[("tensor", t_rb)])
        Q2 = mk.ar2[0:64, 3 * T + 384:3 * T + 512]
        v_q2 = mk.op("vector", lambda e, i0=i0: e.tensor_tensor(
            out=Q2, in0=QF[0:64, i0:i0 + CCH], in1=mk.pb[4][0:64, 0:CCH],
            op=ALU.mult))
        for dvt in range(KT):
            wl = [("vector", v_q2)]
            if dvt >= 2:
                wl.append(("scalar", mk.last("scalar")))
            if cc > 0:
                mk.op("tensor", lambda e, dvt=dvt: e.matmul(
                    out=mk.pb[1 + (dvt % 2)][:, 0:CCH],
                    lhsT=KV16[0:64, dvt * 128:(dvt + 1) * 128],
                    rhs=Q2, start=True, stop=False), waits=wl)
            mk.op("tensor", lambda e, dvt=dvt, cc=cc: e.matmul(
                out=mk.pb[1 + (dvt % 2)][:, 0:CCH],
                lhsT=mk.arena[:, cc * 1024 + dvt * 128:cc * 1024 + (dvt + 1) * 128],
                rhs=ATM, start=(cc == 0), stop=True),
                waits=wl if cc == 0 else (), inc=1)
            t_n = mk.last("tensor")
            mk.op("scalar", lambda e, dvt=dvt: e.activation(
                out=YST[:, dvt * 128:(dvt + 1) * 128],
                in_=mk.pb[1 + (dvt % 2)][:, 0:CCH], func=AF.Copy),
                waits=[("tensor", t_n)])
        s_y = mk.last("scalar")
        if cc == 0:
            mk.dump("ab_yst0", YST[:, 0:1024], [("scalar", s_y)])
        mk.dma("scalar", ots.rearrange("(dvt p) f -> p dvt f", p=128)
               [:, :, i0:i0 + CCH], YST[:, 0:1024].rearrange(
                   "p (dvt f) -> p dvt f", dvt=8), waits=[("scalar", s_y)])
        t_u1 = mk.op("tensor", lambda e, cc=cc: e.matmul(
            out=mk.pb[6][0:64, :], lhsT=KTOK[:, cc * 64:(cc + 1) * 64],
            rhs=mk.arena[:, cc * 1024:cc * 1024 + 512],
            start=True, stop=True), waits=[("scalar", s_y)])
        t_u2 = mk.op("tensor", lambda e, cc=cc: e.matmul(
            out=mk.pb[7][0:64, :], lhsT=KTOK[:, cc * 64:(cc + 1) * 64],
            rhs=mk.arena[:, cc * 1024 + 512:cc * 1024 + 1024],
            start=True, stop=True))
        t_u3 = mk.op("tensor", lambda e, cc=cc: e.matmul(
            out=mk.pb[5][0:64, 0:1], lhsT=KTOK[:, cc * 64:(cc + 1) * 64],
            rhs=cs(mk, CS_ONECOL, 1),
            start=True, stop=True), inc=1)
        t_u = mk.last("tensor")
        if cc == 0:
            mk.op("vector", lambda e: e.tensor_copy(out=KVF[0:64, 0:512],
                                                    in_=mk.pb[6][0:64, :]),
                  waits=[("tensor", t_u)])
            mk.op("vector", lambda e: e.tensor_copy(out=KVF[0:64, 512:1024],
                                                    in_=mk.pb[7][0:64, :]))
            mk.op("vector", lambda e: e.tensor_copy(out=KSF[0:64, 0:1],
                                                    in_=mk.pb[5][0:64, 0:1]),
                  inc=1)
        else:
            mk.op("vector", lambda e: e.tensor_add(out=KVF[0:64, 0:512],
                                                   in0=KVF[0:64, 0:512],
                                                   in1=mk.pb[6][0:64, :]),
                  waits=[("tensor", t_u)])
            mk.op("vector", lambda e: e.tensor_add(out=KVF[0:64, 512:1024],
                                                   in0=KVF[0:64, 512:1024],
                                                   in1=mk.pb[7][0:64, :]))
            mk.op("vector", lambda e: e.tensor_add(out=KSF[0:64, 0:1],
                                                   in0=KSF[0:64, 0:1],
                                                   in1=mk.pb[5][0:64, 0:1]),
                  inc=1)
        la_done = mk.last("tensor")

    for k in range(KT):
        mk.dma("sync", mk.sq[:, k * 2048:(k + 1) * 2048],
               ots[k * 128:(k + 1) * 128, :],
               waits=[("tensor", la_done), ("scalar", mk.last("scalar"))]
               if k == 0 else ())
    ot_ld = mk.last("sync")

    def o_ap(nt):
        return wsrc3(mk.dram["ab_oT"][:, nt * 128:(nt + 1) * 128])

    def o_out(nt, c, bank, t_mm):
        dst = mk.rk(nt)[:, c * FCH:(c + 1) * FCH]
        bcol = OBC[:, nt:nt + 1]
        m1_ = mk.op("vector", lambda e, dst=dst, bank=bank, bcol=bcol:
                    e.scalar_tensor_tensor(out=dst, in0=bank[:, :], scalar=bcol,
                                           op0=ALU.add, in1=dst, op1=ALU.add),
                    waits=[("tensor", t_mm)])
        return ("vector", m1_)

    emit_gemm_b(mk, o_ap, 8, lambda k: mk.sq[:, k * 2048:(k + 1) * 2048],
                o_out, x_wait=[("sync", ot_ld),
                               ("vector", mk.last("vector"))],
                cast_eng="scalar")
    w = [("vector", mk.last("vector"))]
    for k in range(KT):
        mk.dump(f"ab_post_o{k}", mk.rk(k), w)

    LN2G = mk.arena[0:1, 12800:12800 + D]
    LN2B = mk.arena[0:1, 12800 + D:12800 + 2 * D]
    ms2 = None
    for name, dst in (("ab_ln2_w", LN2G), ("ab_ln2_b", LN2B)):
        ld = mk.dma("sync", mk.strow[0:1, 0:D],
                    mk.dram[name].rearrange("(o f) -> o f", o=1),
                    waits=(list(w) if ms2 is None else [("vector", ms2)]) +
                    [("vector", mk.last("vector"))])
        ms2 = mk.op("vector", lambda e, dst=dst: e.tensor_copy(
            out=dst, in_=mk.strow[0:1, 0:D]), waits=[("sync", ld)])
    cd2 = emit_cast_shift(mk, [mk.rk(k) for k in range(KT)], mk.sq,
                          shifted=False, waits=w)
    v2 = emit_ln(mk, mk.sq, mk.xb, g_row=LN2G, b_row=LN2B, cast_done=cd2,
                 extra_wait=[("vector", ms2)])
    x_wait = [("vector", v2)]
    f1 = mk.dram["ab_f1T"]
    f2 = mk.dram["ab_f2T"]
    for tch in range(4):
        tok0 = tch * FCH
        for nt in range(32):
            wbuf, cw = mk.ws.load(
                [(wdst3(mk.wst[:, 0:1024], KT),
                  wsrc3(f1[:, nt * 128:(nt + 1) * 128]))], 1024,
                extra_wait=list(w))
            bank = mk.pb[nt % 2]
            for k in range(KT):
                mk.op("tensor", lambda e, k=k, nt=nt, bank=bank, tok0=tok0,
                      wbuf=wbuf:
                      e.matmul(out=bank[:, :],
                               lhsT=wbuf[:, k * 128:(k + 1) * 128],
                               rhs=mk.bk(mk.xb, k)[:, tok0:tok0 + FCH],
                               start=(k == 0), stop=(k == KT - 1)),
                      waits=([("scalar", mk.last("scalar")), ("scalar", cw)]
                             + x_wait)
                      if k == 0 else (), inc=(1 if k == KT - 1 else 0))
            t_mm = mk.last("tensor")
            mk.ws.consumed(t_mm)
            mk.op("scalar", lambda e, nt=nt, bank=bank: e.activation(
                out=mk.arena[:, nt * FCH:(nt + 1) * FCH], in_=bank[:, :],
                func=AF.Gelu, bias=F1BC[:, nt:nt + 1]),
                waits=[("tensor", t_mm)])
        h_done = mk.last("scalar")
        for nt in range(KT):
            bank = mk.pb[4 + (nt % 2)]
            for piece in range(2):
                wbuf, cw = mk.ws.load(
                    [(wdst3(mk.wst[:, 0:2048], 16),
                      wsrc3(f2[piece * 2048:(piece + 1) * 2048,
                               nt * 128:(nt + 1) * 128]))], 2048)
                for k in range(16):
                    kk = piece * 16 + k
                    mk.op("tensor", lambda e, k=k, kk=kk, bank=bank, wbuf=wbuf:
                          e.matmul(out=bank[:, :],
                                   lhsT=wbuf[:, k * 128:(k + 1) * 128],
                                   rhs=mk.arena[:, kk * FCH:(kk + 1) * FCH],
                                   start=(kk == 0), stop=(kk == 31)),
                          waits=[("scalar", cw), ("scalar", h_done),
                                 ("vector", mk.last("vector"))]
                          if k == 0 else (), inc=(1 if k == 15 else 0))
                mk.ws.consumed(mk.last("tensor"))
            t_mm = mk.last("tensor")
            dst = mk.rk(nt)[:, tok0:tok0 + FCH]
            bcol = F2BC[:, nt:nt + 1]
            mk.op("vector", lambda e, dst=dst, bank=bank, bcol=bcol:
                  e.scalar_tensor_tensor(out=dst, in0=bank[:, :], scalar=bcol,
                                         op0=ALU.add, in1=dst, op1=ALU.add),
                  waits=[("tensor", t_mm)])
    w = [("vector", mk.last("vector"))]
    vu = emit_unpark_add(mk, w)
    return [("vector", vu)]



def host_consts():
    import ml_dtypes
    bf = lambda a: np.ascontiguousarray(a.astype(ml_dtypes.bfloat16))
    c = {}
    c["c_ident16"] = bf(np.eye(128))
    small = np.zeros((128, 960), np.float64)
    small[:, CS_ONECOL] = 1.0
    small[0:64, CS_SEL2] = 1.0
    small[64:128, CS_SEL2 + 1] = 1.0
    for p in range(128):
        small[p // 64, CS_C2T + p] = 1.0
        small[p % 64, CS_KBSEL + p] = 1.0
        small[(p + 32) % 64, CS_KBSELP + p] = 1.0
    small[0:1, CS_ONEROW:CS_ONEROW + FCH] = 1.0
    c["c_small16"] = bf(small)

    masks = np.zeros((128, 1664), np.float64)
    for p in range(128):
        for g in range(640):
            d = 384 + p - g
            if not (0 <= d <= 256):
                masks[p, g] = NEG
    for p in range(128):
        for g in range(384):
            if not (p >= g - 128):
                masks[p, 640 + g] = NEG
    for p in range(128):
        for g in range(640):
            d = 256 + (g - 384) - p
            if not (0 <= d <= 256):
                masks[p, 1024 + g] = NEG
    c["c_masks16"] = bf(masks)

    perm = np.zeros((128, 128), np.float64)
    for a in range(128):
        g, r = a // 64, a % 64
        psrc = g * 64 + ((r + 32) % 64)
        perm[psrc, a] = 1.0
    c["c_perm16"] = bf(perm)
    vbsel = np.zeros((128, 4096), np.float64)
    for t in range(32):
        for p in range(128):
            dd = 2 * t + p // 64
            vbsel[dd, t * 128 + p] = 1.0
    vbsel[64:128] = vbsel[0:64]
    c["c_vbsel16"] = bf(vbsel)
    msum = np.zeros((128, 2048), np.float64)
    for t in range(32):
        for p in range(128):
            msum[p, t * 64 + 2 * t + p // 64] = 1.0
    c["c_msum16"] = bf(msum)
    kbr = np.zeros((128, 128), np.float64)
    for p in range(128):
        kbr[p % 64, p] = 1.0
        kbr[64 + p % 64, p] = 1.0
    c["c_kbselr16"] = bf(kbr)
    tri = np.zeros((128, 128), np.float64)
    for s in range(128):
        tri[s, s:] = 1.0
    c["c_trim16"] = bf(tri)

    half = np.arange(0, DK, 2, dtype=np.float64)
    inv_freq = 1.0 / 10000.0 ** (half / DK)
    t = np.arange(T, dtype=np.float64)
    freqs = t[:, None] * inv_freq[None, :]
    freqs = np.concatenate([freqs, freqs], -1)
    base_scale = (half + 0.4 * DK) / (1.4 * DK)
    power = (t - T // 2) / (WIN // 2)
    scale = base_scale[None, :] ** power[:, None]
    scale = np.concatenate([scale, scale], -1)
    cosf, sinf = np.cos(freqs), np.sin(freqs)
    sgn = np.ones((T, 64))
    sgn[:, :32] = -1.0
    csq = (cosf * scale).T
    ssq = (sgn * sinf * scale).T
    csk = (cosf / scale).T
    ssk = (sgn * sinf / scale).T
    tab = np.concatenate([csq, ssq, csk, ssk], axis=1)
    tab = np.concatenate([tab, tab], axis=0)
    c["c_tab16"] = bf(tab)

    fsmall = np.zeros((128, 8), np.float32)
    fsmall[:, 0] = LN_EPS
    fsmall[:, 1] = 1.0
    fsmall[:, 2] = 1e-24
    fsmall[:, 3] = 1e-6
    c["c_fsmall"] = fsmall
    return c



def prepare_in_maps(mk, ins, ncore=8):
    x = np.ascontiguousarray(ins["L0_attn_a_x"])
    wmap = {}
    wmap.update(host_consts())
    wmap["aa_wqkvT"] = np.ascontiguousarray(ins["L0_attn_a_W_qkv"].T)
    wmap["aa_woT"] = np.ascontiguousarray(ins["L0_attn_a_W_o"].T)
    wmap["aa_wltm"] = np.ascontiguousarray(ins["L0_attn_a_W_LTM"])
    wmap["aa_wltmT"] = np.ascontiguousarray(
        ins["L0_attn_a_W_LTM"].transpose(0, 2, 1))
    for nm in ("V_gs", "V_T0", "beta_tau", "beta_gm", "C_ch", "gamma",
               "alpha_ppd", "IC_threshold"):
        wmap["aa_" + nm] = np.ascontiguousarray(ins["L0_attn_a_" + nm])
    wmap["ab_qT"] = np.ascontiguousarray(ins["L0_attn_b_q_w"].T)
    wmap["ab_kT"] = np.ascontiguousarray(ins["L0_attn_b_k_w"].T)
    wmap["ab_vT"] = np.ascontiguousarray(ins["L0_attn_b_v_w"].T)
    wmap["ab_oT"] = np.ascontiguousarray(ins["L0_attn_b_o_w"].T)
    wmap["ab_f1T"] = np.ascontiguousarray(ins["L0_attn_b_f1_w"].T)
    wmap["ab_f2T"] = np.ascontiguousarray(ins["L0_attn_b_f2_w"].T)
    for nm in ("ln1_w", "ln1_b", "ln2_w", "ln2_b", "q_b", "k_b", "v_b",
               "o_b", "f1_b", "f2_b"):
        wmap["ab_" + nm] = np.ascontiguousarray(ins["L0_attn_b_" + nm])
    for nm in ("Wqkv", "q_scale", "k_scale", "Wo", "ln_g", "ln_b", "W1",
               "b1", "W2", "b2"):
        wmap["ff_" + nm] = np.ascontiguousarray(ins["L0_ffn_" + nm])

    in_maps = []
    for ci in range(ncore):
        m = {}
        for name in mk.dram:
            if name == "x_t":
                m[name] = np.ascontiguousarray(x[ci].T)
            elif name in wmap:
                m[name] = wmap[name]
            elif name in mk.consts:
                m[name] = mk.consts[name]
        in_maps.append(m)
    return in_maps


def kernel(**ins):
    ncore = 8
    mk = build_program(debug_dump=tuple(DEBUG["dump"]), blocks=DEBUG["blocks"],
                       fdepth=DEBUG["fdepth"])
    in_maps = prepare_in_maps(mk, ins, ncore)
    nc = mk.build()
    res = run_bass_kernel_spmd(nc, in_maps, core_ids=list(range(ncore)))
    outs = []
    for ci in range(ncore):
        outs.append(np.ascontiguousarray(res.results[ci]["out_t"].T))
    out = np.stack(outs, axis=0).astype(np.float32)
    if mk.dumped:
        kernel._dbg = [{d: res.results[ci][d] for d in mk.dumped}
                       for ci in range(ncore)]
    return out
